# version 07
# baseline (speedup 1.0000x reference)
import numpy as np
import ml_dtypes
import os
import contextlib
from collections import defaultdict

import concourse.bass as bass
import concourse.mybir as mybir
from concourse.bass_utils import run_bass_kernel_spmd

F32 = mybir.dt.float32
BF16 = mybir.dt.bfloat16
I32 = mybir.dt.int32
AF = mybir.ActivationFunctionType
ALU = mybir.AluOpType
AX = mybir.AxisListType
BF = ml_dtypes.bfloat16

PI = float(np.pi)
INV2PI = float(1.0 / (2 * np.pi))
N2PI = float(-2 * np.pi)

N_CORES = 8
B_PER = 2
L = 2048
D = 2048
T = B_PER * L
CH = 256
NCH = L // CH
KD = D // 128
MEM = 288
NQ = 512
NMLP = 16384
GELU_C1 = 0.7978845608028654
GELU_C3 = float(0.7978845608028654 * 0.044715)

ENGS = ("sync", "tensor", "vector", "scalar", "gpsimd")
P2S = int(os.environ.get("K_P2S", "9"))
DBG = int(os.environ.get("K_DBG", "0"))
P4NM = int(os.environ.get("K_P4NM", "8"))


class Prog:
    def __init__(self):
        self.q = {e: [] for e in ENGS}
        self.C = defaultdict(int)
        self.sem_names = set()

    def ev(self, e, fn, waits=(), inc=None):
        if inc is not None:
            self.sem_names.add(inc[0])
            self.C[inc[0]] += inc[1]
            tok = self.C[inc[0]]
        else:
            tok = None
        for s, v in waits:
            self.sem_names.add(s)
        self.q[e].append((fn, list(waits), inc))
        return tok

    def w(self, e, waits):
        for s, v in waits:
            self.sem_names.add(s)
        self.q[e].append((None, list(waits), None))

    def replay(self, e, eng, sems):
        pend = []
        for fn, waits, inc in self.q[e]:
            pend.extend(waits)
            if fn is None:
                continue
            for s, v in pend:
                eng.wait_ge(sems[s], v)
            pend = []
            r = fn(eng)
            if inc is not None:
                r.then_inc(sems[inc[0]], inc[1])
        assert not pend, f"dangling waits on {e}"


def build_program(upto=9):
    nc = bass.Bass("TRN2", target_bir_lowering=False)
    p = Prog()
    C = p.C

    def din(name, shape, dt):
        return nc.dram_tensor(name, shape, dt, kind="ExternalInput").ap()

    x_in = din("x", [T, D], F32)
    wmem_in = din("wmem", [D, MEM], F32)
    convkt_in = din("convkt", [384, 4], F32)
    wq_in = din("wq", [D, NQ], F32)
    theta_in = din("theta", [256, 8], F32)
    slopes_in = din("slopes", [32, 1], F32)
    score_in = din("score", [32, 1], F32)
    rebd_in = din("rebd", [256, 1536], F32)
    imbd_in = din("imbd", [256, 1536], F32)
    ns_in = din("nsc", [256, 1], F32)
    wout_in = din("wout", [768, D], F32)
    posdist_in = din("posdist", [32, L], F32)
    selpk_in = din("selpk", [32, 2, 128], F32)
    selq_in = din("selq", [128, 4096], F32)
    gpk_in = din("gpk", [NMLP, D // 2], I32)
    gsc_in = din("gsc", [NMLP, D // 32], F32)
    upk_in = din("upk", [NMLP, D // 2], I32)
    usc_in = din("usc", [NMLP, D // 32], F32)
    dpk_in = din("dpk", [D, NMLP // 2], I32)
    dsc_in = din("dsc", [D, NMLP // 32], F32)

    out_d = nc.dram_tensor("out", [T, D], F32, kind="ExternalOutput").ap()
    dbg = {}
    if DBG:
        for nm, shape, dt in (("dbgA", [128, 3, CH], F32), ("dbgB", [128, 6, CH], F32),
                              ("dbgC", [128, 4, CH], F32), ("dbgC2", [128, 4, CH], F32),
                              ("dbgD", [128, 2, CH], F32), ("dbgE", [128, 6, CH], F32),
                              ("dbgF", [128, D], F32), ("dbgQ", [128, 4, CH], F32),
                              ("dbgQX", [128, 4, CH], F32), ("dbgG", [256, D], BF16),
                              ("dbgH", [256, D], F32), ("dbgI", [128, D], BF16),
                              ("dbgJ", [128, T], BF16), ("dbgK", [256, D], F32)):
            dbg[nm] = nc.dram_tensor(nm, shape, dt, kind="ExternalOutput").ap()

    def dint(name, shape, dt):
        return nc.dram_tensor(name, shape, dt, kind="Internal").ap()

    wg_d = dint("wg_d", [NMLP, D], BF16)
    wu_d = dint("wu_d", [NMLP, D], BF16)
    wd_d = dint("wd_d", [D, NMLP], BF16)
    xln1_d = dint("xln1_d", [T, D], BF16)
    xln3_d = dint("xln3_d", [T, D], BF16)
    x1_d = dint("x1_d", [T, D], F32)
    inter_d = dint("inter_d", [NMLP, T], BF16)
    out2_d = dint("out2_d", [T, D], F32)

    stack = contextlib.ExitStack()
    ec = stack.enter_context

    cst = ec(nc.sbuf_tensor("cst", [128, 12], F32))
    imagic = ec(nc.sbuf_tensor("imagic", [128, 1], I32))
    theta_sb = ec(nc.sbuf_tensor("theta_sb", [128, 2, 8], F32))
    convkt_sb = ec(nc.sbuf_tensor("convkt_sb", [128, 3, 4], F32))
    slopes_sb = ec(nc.sbuf_tensor("slopes_sb", [128, 1], F32))
    score_sb = ec(nc.sbuf_tensor("score_sb", [128, 1], F32))
    sp_sb = ec(nc.sbuf_tensor("sp_sb", [128, 1], F32))
    tiny = ec(nc.sbuf_tensor("tiny", [128, 8], F32))
    logw_sb = ec(nc.sbuf_tensor("logw_sb", [128, L], F32))
    selpk_sb = ec(nc.sbuf_tensor("selpk_sb", [32, 2, 128], F32))
    stat = ec(nc.sbuf_tensor("stat", [128, 16], F32))
    stat2 = ec(nc.sbuf_tensor("stat2", [128, 16], F32))
    rstd_sb = ec(nc.sbuf_tensor("rstd_sb", [128, 8], F32))
    bias_sb = ec(nc.sbuf_tensor("bias_sb", [128, 8], F32))
    nwt = ec(nc.sbuf_tensor("nwt", [128, 8], F32))
    nwt2 = ec(nc.sbuf_tensor("nwt2", [128, 8], F32))
    nwi = ec(nc.sbuf_tensor("nwi", [128, 8], I32))
    carry_re = ec(nc.sbuf_tensor("carry_re", [128, 16], F32))
    carry_im = ec(nc.sbuf_tensor("carry_im", [128, 16], F32))
    carry_den = ec(nc.sbuf_tensor("carry_den", [128, 1], F32))

    ps = [ec(nc.psum_tensor(f"ps{i}", [128, 512], F32)) for i in range(8)]
    PS_GEN = (0, 1, 2, 3)
    PS_PW = 4
    PS_ID = 5
    PS_QX = (6, 7)

    def vop(fn, waits=()):
        return p.ev("vector", fn, waits=list(waits), inc=("vg", 1))

    def vgop(fn, waits=()):
        return p.ev("vector", fn, waits=[("vg", C["vg"])] + list(waits), inc=("vg", 1))

    def newton_rsqrt(n, var_ap_fn, out_ap_fn):
        vgop(lambda e: e.tensor_single_scalar(nwt[:, 0:n], var_ap_fn(), cst[:, 5:6], ALU.add))
        vgop(lambda e: e.tensor_copy(nwi[:, 0:n], nwt[:, 0:n].bitcast(I32)))
        vgop(lambda e: e.tensor_single_scalar(nwi[:, 0:n], nwi[:, 0:n], 1, ALU.logical_shift_right))
        vgop(lambda e: e.tensor_tensor(nwi[:, 0:n], imagic[:].to_broadcast([128, n]), nwi[:, 0:n], ALU.subtract))
        vgop(lambda e: e.tensor_copy(out_ap_fn(), nwi[:, 0:n].bitcast(F32)))
        for _ in range(4):
            vgop(lambda e: e.tensor_mul(nwt2[:, 0:n], out_ap_fn(), out_ap_fn()))
            vgop(lambda e: e.tensor_mul(nwt2[:, 0:n], nwt2[:, 0:n], nwt[:, 0:n]))
            vgop(lambda e: e.tensor_scalar(nwt2[:, 0:n], nwt2[:, 0:n], -0.5, 1.5, ALU.mult, ALU.add))
            vgop(lambda e: e.tensor_mul(out_ap_fn(), out_ap_fn(), nwt2[:, 0:n]))

    def mm(bank_ap, lhsT, rhs, start, stop, inc=None, waits=()):
        return p.ev("tensor", (lambda e, bank_ap=bank_ap, lhsT=lhsT, rhs=rhs, start=start, stop=stop:
                               e.matmul(bank_ap, lhsT, rhs, start=start, stop=stop)),
                    waits=waits, inc=inc)

    p2 = contextlib.ExitStack()
    ec2 = p2.enter_context
    wmem_b = ec2(nc.sbuf_tensor("wmem_b", [128, KD, MEM], BF16))
    wq_b = ec2(nc.sbuf_tensor("wq_b", [128, KD, NQ], BF16))
    wout_b = ec2(nc.sbuf_tensor("wout_b", [128, 6, D], BF16))
    rebd_b = ec2(nc.sbuf_tensor("rebd_b", [128, 2, 1536], BF16))
    imbd_b = ec2(nc.sbuf_tensor("imbd_b", [128, 2, 1536], BF16))
    selq_sb = ec2(nc.sbuf_tensor("selq_sb", [128, 4096], F32))

    p1s = contextlib.ExitStack()
    ec1 = p1s.enter_context
    sqsc0 = ec1(nc.sbuf_tensor("sqsc0", [128, D], F32))

    for k in range(KD):
        t = p.ev("sync", (lambda e, k=k: e.dma_start(out=sqsc0[:, 0:MEM], in_=wmem_in[128 * k:128 * (k + 1), :])), inc=("ld", 16))
        tv = p.ev("vector", (lambda e, k=k: e.tensor_copy(wmem_b[:, k, :], sqsc0[:, 0:MEM])),
                  waits=[("ld", t)], inc=("vg", 1))
        p.w("sync", [("vg", tv)])
    for k in range(KD):
        t = p.ev("sync", (lambda e, k=k: e.dma_start(out=sqsc0[:, 0:NQ], in_=wq_in[128 * k:128 * (k + 1), :])), inc=("ld", 16))
        tv = p.ev("vector", (lambda e, k=k: e.tensor_copy(wq_b[:, k, :], sqsc0[:, 0:NQ])),
                  waits=[("ld", t)], inc=("vg", 1))
        p.w("sync", [("vg", tv)])
    for g in range(6):
        t = p.ev("sync", (lambda e, g=g: e.dma_start(out=sqsc0[:, 0:D], in_=wout_in[128 * g:128 * (g + 1), :])), inc=("ld", 16))
        tv = p.ev("vector", (lambda e, g=g: e.tensor_copy(wout_b[:, g, :], sqsc0[:, 0:D])),
                  waits=[("ld", t)], inc=("vg", 1))
        p.w("sync", [("vg", tv)])
    t_ns = p.ev("sync", lambda e: e.dma_start(out=tiny[:, 0:2], in_=ns_in.rearrange("(g q) o -> q (g o)", q=128), allow_slow_non_contiguous=True), inc=("ld", 16))
    for src, dst in ((rebd_in, rebd_b), (imbd_in, imbd_b)):
        for g in range(2):
            t = p.ev("sync", (lambda e, src=src, g=g: e.dma_start(out=sqsc0[:, 0:1536], in_=src[128 * g:128 * (g + 1), :])), inc=("ld", 16))
            vgop((lambda e, g=g: e.tensor_single_scalar(sqsc0[:, 0:1536], sqsc0[:, 0:1536], tiny[:, g:g + 1], ALU.mult)),
                 waits=[("ld", t), ("ld", t_ns)])
            tv = vgop(lambda e, dst=dst, g=g: e.tensor_copy(dst[:, g, :], sqsc0[:, 0:1536]))
            p.w("sync", [("vg", tv)])
    p.ev("sync", lambda e: e.dma_start(out=theta_sb[:], in_=theta_in.rearrange("(g q) m -> q g m", q=128)), inc=("ld", 16))
    p.ev("sync", lambda e: e.dma_start(out=convkt_sb[:], in_=convkt_in.rearrange("(g q) w -> q g w", q=128)), inc=("ld", 16))
    p.ev("sync", lambda e: e.dma_start(out=slopes_sb[0:32, :], in_=slopes_in), inc=("ld", 16))
    p.ev("sync", lambda e: e.dma_start(out=score_sb[0:32, :], in_=score_in), inc=("ld", 16))
    p.ev("sync", lambda e: e.dma_start(out=selpk_sb[:], in_=selpk_in), inc=("ld", 16))
    p.ev("sync", lambda e: e.dma_start(out=selq_sb[:], in_=selq_in), inc=("ld", 16))
    t_pos = p.ev("sync", lambda e: e.dma_start(out=logw_sb[0:32, :], in_=posdist_in), inc=("ld", 16))
    tld_all = C["ld"]

    for col, val in ((0, PI / 2), (1, 0.5), (2, -4.0), (3, -6.0), (4, 1.0 / 2048.0),
                     (5, 1e-5), (7, 1e-4), (10, -1.0), (11, 1.0)):
        vgop(lambda e, col=col, val=val: e.memset(cst[:, col:col + 1], float(val)))
    vgop(lambda e: e.memset(imagic[:], 0x5F3759DF))

    vgop(lambda e: e.tensor_scalar_mul(tiny[:, 2:3], slopes_sb[:, 0:1], -1.0), waits=[("ld", tld_all)])
    vgop(lambda e: e.tensor_max(tiny[:, 3:4], slopes_sb[:, 0:1], tiny[:, 2:3]))
    vgop(lambda e: e.tensor_scalar_max(tiny[:, 4:5], slopes_sb[:, 0:1], 0.0))
    tsp = C["vg"]
    p.ev("scalar", lambda e: e.preload_act_table(AF.Exp), inc=("ac", 1))
    p.ev("scalar", lambda e: e.activation(tiny[:, 5:6], tiny[:, 3:4], AF.Exp, scale=cst[:, 10:11]),
         waits=[("vg", tsp)], inc=("ac", 1))
    p.ev("scalar", lambda e: e.preload_act_table(AF.Ln), inc=("ac", 1))
    p.ev("scalar", lambda e: e.activation(tiny[:, 6:7], tiny[:, 5:6], AF.Ln, bias=cst[:, 11:12]), inc=("ac", 1))
    tspa = C["ac"]
    vgop(lambda e: e.tensor_add(sp_sb[:, 0:1], tiny[:, 4:5], tiny[:, 6:7]), waits=[("ac", tspa)])
    vgop(lambda e: e.tensor_scalar(logw_sb[0:32, :], logw_sb[0:32, :], sp_sb[0:32, 0:1], -1.0, ALU.mult, ALU.mult),
         waits=[("ld", t_pos)])
    p.ev("scalar", lambda e: e.preload_act_table(AF.Silu), waits=[("ac", C["ac"])], inc=("ac", 1))

    p1s.close()
    p.w("sync", [("vg", C["vg"]), ("ac", C["ac"])])
    ck = contextlib.ExitStack()
    eck = ck.enter_context
    sx = eck(nc.sbuf_tensor("sx", [128, 2, D], F32))
    xlnst = eck(nc.sbuf_tensor("xlnst", [128, 2, D], BF16))
    xfm = eck(nc.sbuf_tensor("xfm", [128, KD, CH], BF16))
    zbuf = eck(nc.sbuf_tensor("zbuf", [128, 3, CH + 3], F32))
    zconv = eck(nc.sbuf_tensor("zconv", [128, 3, CH], F32))
    pwv = eck(nc.sbuf_tensor("pwv", [128, 6, CH], F32))
    qsb = eck(nc.sbuf_tensor("qsb", [128, 4, CH], F32))
    trA = eck(nc.sbuf_tensor("trA", [128, 4, CH], F32))
    trB = eck(nc.sbuf_tensor("trB", [128, 4, CH], F32))
    trC = eck(nc.sbuf_tensor("trC", [128, 4, CH], F32))
    trD = eck(nc.sbuf_tensor("trD", [128, 4, CH], F32))
    trNI = eck(nc.sbuf_tensor("trNI", [128, 4, CH], I32))
    qxre = eck(nc.sbuf_tensor("qxre", [128, 4, CH], F32))
    qxim = eck(nc.sbuf_tensor("qxim", [128, 4, CH], F32))
    orsb = eck(nc.sbuf_tensor("orsb", [128, 2, CH], F32))
    oisb = eck(nc.sbuf_tensor("oisb", [128, 2, CH], F32))
    orb = eck(nc.sbuf_tensor("orb", [128, 2, CH], BF16))
    oib = eck(nc.sbuf_tensor("oib", [128, 2, CH], BF16))
    yact = eck(nc.sbuf_tensor("yact", [128, 6, CH], BF16))
    silu_t = eck(nc.sbuf_tensor("silu_t", [128, CH], F32))
    asb = eck(nc.sbuf_tensor("asb", [128, D], F32))
    sqsc = eck(nc.sbuf_tensor("sqsc", [128, D], F32))
    x1sb = eck(nc.sbuf_tensor("x1sb", [128, D], F32))
    xl3st = eck(nc.sbuf_tensor("xl3st", [128, D], BF16))

    for b in range(B_PER if upto >= 2 else 0):
        vgop(lambda e: e.memset(zbuf[:, :, 0:3], 0.0))
        vgop(lambda e: e.memset(carry_re[:], 0.0))
        vgop(lambda e: e.memset(carry_im[:], 0.0))
        vgop(lambda e: e.memset(carry_den[:], 0.0))
        for c in range(NCH):
            row0 = b * L + c * CH
            for tt_ in range(2):
                r0 = row0 + 128 * tt_
                tl = p.ev("sync", (lambda e, r0=r0, tt_=tt_: e.dma_start(out=sx[:, tt_, :], in_=x_in[r0:r0 + 128, :])),
                          inc=("ld", 16))
                p.ev("vector", (lambda e, tt_=tt_: e.tensor_reduce(stat[:, tt_:tt_ + 1], sx[:, tt_, :], AX.X, ALU.add)),
                     waits=[("ld", tl), ("vg", C["vg"])], inc=("vg", 1))
                p.ev("scalar", (lambda e, tt_=tt_: e.activation(sqsc[:], sx[:, tt_, :], AF.Square, accum_out=stat2[:, tt_:tt_ + 1])),
                     waits=[("ld", tl), ("ac", C["ac"])], inc=("ac", 1))
            tstat_a = C["ac"]
            vgop(lambda e: e.tensor_single_scalar(stat[:, 0:2], stat[:, 0:2], cst[:, 4:5], ALU.mult))
            vgop(lambda e: e.tensor_single_scalar(stat2[:, 0:2], stat2[:, 0:2], cst[:, 4:5], ALU.mult),
                 waits=[("ac", tstat_a)])
            vgop(lambda e: e.tensor_mul(nwt2[:, 0:2], stat[:, 0:2], stat[:, 0:2]))
            vgop(lambda e: e.tensor_tensor(stat2[:, 0:2], stat2[:, 0:2], nwt2[:, 0:2], ALU.subtract))
            newton_rsqrt(2, lambda: stat2[:, 0:2], lambda: rstd_sb[:, 0:2])
            vgop(lambda e: e.tensor_mul(bias_sb[:, 0:2], stat[:, 0:2], rstd_sb[:, 0:2]))
            tln1 = vgop(lambda e: e.tensor_scalar_mul(bias_sb[:, 0:2], bias_sb[:, 0:2], -1.0))
            for tt_ in range(2):
                tap = p.ev("scalar", (lambda e, tt_=tt_: e.activation(xlnst[:, tt_, :], sx[:, tt_, :], AF.Prelu, alpha=1.0,
                                                                      scale=rstd_sb[:, tt_:tt_ + 1], bias=bias_sb[:, tt_:tt_ + 1])),
                           waits=[("vg", tln1), ("gd", C["gd"]), ("ac", C["ac"])], inc=("ac", 1))
                p.ev("gpsimd", (lambda e, tt_=tt_, r0=row0 + 128 * tt_: e.dma_start(out=xln1_d[r0:r0 + 128, :], in_=xlnst[:, tt_, :])),
                     waits=[("ac", tap)], inc=("gd", 16))
            txw = C["gd"]
            for k in range(KD):
                p.ev("sync", (lambda e, k=k, row0=row0: e.dma_start_transpose(xfm[:, k, :], xln1_d[row0:row0 + CH, 128 * k:128 * (k + 1)])),
                     waits=([("gd", txw)] if k == 0 else ()), inc=("ld", 16))
            txf = C["ld"]

            if P2S < 2:
                continue
            mm_w = [("ld", txf), ("vg", C["vg"]), ("ac", C["ac"])]
            for mc, (m0, msz) in enumerate(((0, 128), (128, 128), (256, 32))):
                for k in range(KD):
                    mm(ps[PS_GEN[mc]][0:msz, 0:CH], wmem_b[:, k, m0:m0 + msz], xfm[:, k, :],
                       start=(k == 0), stop=(k == KD - 1),
                       waits=(mm_w if (mc == 0 and k == 0) else ()),
                       inc=(("tn", 1) if k == KD - 1 else None))
            tz = C["tn"]
            for mc, (m0, msz) in enumerate(((0, 128), (128, 128), (256, 32))):
                p.ev("vector", (lambda e, mc=mc, msz=msz: e.tensor_copy(zbuf[0:msz, mc, 3:3 + CH], ps[PS_GEN[mc]][0:msz, 0:CH])),
                     waits=[("tn", tz), ("vg", C["vg"])], inc=("vg", 1))
            tzev = C["vg"]
            for g in range(4):
                for k in range(KD):
                    mm(ps[PS_GEN[g]][:, 0:CH], wq_b[:, k, 128 * g:128 * (g + 1)], xfm[:, k, :],
                       start=(k == 0), stop=(k == KD - 1),
                       waits=([("vg", tzev)] if (k == 0 and g == 0) else ()),
                       inc=(("tn", 1) if k == KD - 1 else None))
            tq = C["tn"]
            for g in range(4):
                p.ev("scalar", (lambda e, g=g: e.activation(qsb[:, g, :], ps[PS_GEN[g]][:, 0:CH], AF.Copy)),
                     waits=([("tn", tq)] if g == 0 else ()), inc=("ac", 1))
            tqev = C["ac"]

            if P2S < 3:
                continue
            first_conv = True
            for gz in range(3):
                msz = 32 if gz == 2 else 128
                wts = [("vg", C["vg"]), ("ac", C["ac"])] if first_conv else [("vg", C["vg"])]
                first_conv = False
                p.ev("vector", (lambda e, gz=gz, msz=msz: e.tensor_single_scalar(
                    zconv[0:msz, gz, :], zbuf[0:msz, gz, 0:CH], convkt_sb[0:msz, gz, 0:1], ALU.mult)),
                    waits=wts, inc=("vg", 1))
                for w_ in range(1, 4):
                    vgop(lambda e, gz=gz, w_=w_, msz=msz: e.scalar_tensor_tensor(
                        zconv[0:msz, gz, :], zbuf[0:msz, gz, w_:w_ + CH], convkt_sb[0:msz, gz, w_:w_ + 1], zconv[0:msz, gz, :], ALU.mult, ALU.add))
            vgop(lambda e: e.tensor_copy(zbuf[:, :, 0:3], zbuf[:, :, CH:CH + 3]))
            tconv = C["vg"]

            if P2S < 4:
                continue
            lw0 = c * CH
            vgop(lambda e: e.tensor_single_scalar(pwv[0:32, 0, :], zconv[0:32, 2, :], score_sb[0:32, 0:1], ALU.mult))
            vgop(lambda e: e.tensor_scalar(pwv[0:32, 0, :], pwv[0:32, 0, :], 20.0, -20.0, ALU.min, ALU.max))
            vgop(lambda e, lw0=lw0: e.tensor_tensor(pwv[0:32, 0, :], pwv[0:32, 0, :], logw_sb[0:32, lw0:lw0 + CH], ALU.add))
            vgop(lambda e: e.tensor_scalar_max(pwv[0:32, 0, :], pwv[0:32, 0, :], -80.0))
            tpw1 = vgop(lambda e: e.tensor_copy(pwv[0:32, 1, :], pwv[0:32, 0, :]))
            p.ev("scalar", (lambda e: e.preload_act_table(AF.Exp)),
                 waits=[("ac", C["ac"])], inc=("ac", 1))
            tth = p.ev("scalar", (lambda e: e.activation(pwv[0:32, 0, :], pwv[0:32, 0, :], AF.Exp)),
                       waits=[("vg", tpw1)], inc=("ac", 1))
            p.ev("scalar", (lambda e: e.preload_act_table(AF.Silu)), inc=("ac", 1))
            vgop(lambda e: e.tensor_copy(pwv[0:32, 4, :], pwv[0:32, 0, :]), waits=[("ac", tth)])
            vgop(lambda e: e.tensor_add(pwv[0:32, 4, 0:1], pwv[0:32, 4, 0:1], carry_den[0:32, 0:1]))
            vgop(lambda e: e.tensor_tensor_scan(pwv[0:32, 5, :], pwv[0:32, 4, :], pwv[0:32, 4, :], 0.0, ALU.add, ALU.bypass))
            vgop(lambda e: e.tensor_copy(carry_den[0:32, 0:1], pwv[0:32, 5, CH - 1:CH]))
            vgop(lambda e: e.tensor_single_scalar(pwv[0:32, 4, :], pwv[0:32, 5, :], cst[0:32, 7:8], ALU.max))
            tpw = vgop(lambda e: e.reciprocal(pwv[0:32, 4, :], pwv[0:32, 4, :]))
            if DBG and b == 0 and c == 0:
                p.ev("gpsimd", (lambda e: e.dma_start(out=dbg["dbgA"], in_=zconv[:])), waits=[("vg", tpw)], inc=("gd", 16))
                p.ev("gpsimd", (lambda e: e.dma_start(out=dbg["dbgB"], in_=pwv[:])), inc=("gd", 16))
                p.ev("gpsimd", (lambda e: e.dma_start(out=dbg["dbgQ"], in_=qsb[:])), inc=("gd", 16))
            for kc in range(2):
                mm(ps[PS_PW][:, CH * kc:CH * (kc + 1)], selpk_sb[:, kc, :], pwv[0:32, 0, :],
                   start=True, stop=True, waits=([("vg", tpw)] if kc == 0 else ()), inc=("tn", 1))
                mm(ps[PS_ID][:, CH * kc:CH * (kc + 1)], selpk_sb[:, kc, :], pwv[0:32, 4, :],
                   start=True, stop=True, inc=("tn", 1))
            tsel = C["tn"]

            if P2S < 5:
                continue
            for kc in range(2):
                for mh in range(2):
                    mbase = 4 * mh
                    for i in range(4):
                        p.ev("scalar", (lambda e, kc=kc, i=i, mbase=mbase: e.activation(
                            trA[:, i, :], zconv[:, kc, :], AF.Prelu, alpha=1.0, scale=theta_sb[:, kc, mbase + i:mbase + i + 1])),
                            waits=([("vg", tconv), ("vg", C["vg"]), ("ac", C["ac"])] if i == 0 else ()), inc=("ac", 1))
                    tphi = C["ac"]
                    vop((lambda e: e.tensor_single_scalar(trNI[:], trA[:], INV2PI, ALU.mult)),
                        waits=[("ac", tphi), ("vg", C["vg"])])
                    vop(lambda e: e.tensor_copy(trB[:], trNI[:]))
                    vop(lambda e: e.scalar_tensor_tensor(trB[:], trB[:], N2PI, trA[:], ALU.mult, ALU.add))
                    vop(lambda e: e.tensor_scalar(trNI[:], trA[:], INV2PI, 0.25, ALU.mult, ALU.add))
                    vop(lambda e: e.tensor_copy(trC[:], trNI[:]))
                    targ = vop(lambda e: e.scalar_tensor_tensor(trC[:], trC[:], N2PI, trA[:], ALU.mult, ALU.add))
                    p.ev("scalar", (lambda e: e.activation(trB[:], trB[:], AF.Sin)), waits=[("vg", targ)], inc=("ac", 1))
                    tsin = p.ev("scalar", (lambda e: e.activation(trC[:], trC[:], AF.Sin, bias=cst[:, 0:1])), inc=("ac", 1))
                    pwb = ps[PS_PW][:, CH * kc:CH * (kc + 1)].unsqueeze(1).to_broadcast([128, 4, CH])
                    vop((lambda e, pwb=pwb: e.tensor_tensor(trA[:], trB[:], pwb, ALU.mult)),
                        waits=[("ac", tsin), ("tn", tsel), ("vg", C["vg"])])
                    vop(lambda e, pwb=pwb: e.tensor_tensor(trC[:], trC[:], pwb, ALU.mult))
                    cb = kc * 8 + mbase
                    vgop(lambda e, cb=cb: e.tensor_add(trC[:, :, 0:1], trC[:, :, 0:1], carry_re[:, cb:cb + 4].unsqueeze(2)))
                    vgop(lambda e, cb=cb: e.tensor_add(trA[:, :, 0:1], trA[:, :, 0:1], carry_im[:, cb:cb + 4].unsqueeze(2)))
                    for i in range(4):
                        vgop(lambda e, i=i: e.tensor_tensor_scan(trD[:, i, :], trC[:, i, :], trC[:, i, :], 0.0, ALU.add, ALU.bypass))
                    for i in range(4):
                        vgop(lambda e, i=i: e.tensor_tensor_scan(trB[:, i, :], trA[:, i, :], trA[:, i, :], 0.0, ALU.add, ALU.bypass))
                    vgop(lambda e, cb=cb: e.tensor_copy(carry_re[:, cb:cb + 4].unsqueeze(2), trD[:, :, CH - 1:CH]))
                    tsc2 = vgop(lambda e, cb=cb: e.tensor_copy(carry_im[:, cb:cb + 4].unsqueeze(2), trB[:, :, CH - 1:CH]))
                    if DBG and b == 0 and c == 0 and kc == 0 and mh == 0:
                        p.ev("gpsimd", (lambda e: e.dma_start(out=dbg["dbgC"], in_=trD[:])), waits=[("vg", tsc2)], inc=("gd", 16))
                        p.ev("gpsimd", (lambda e: e.dma_start(out=dbg["dbgC2"], in_=trB[:])), inc=("gd", 16))
                    for ri, qx in ((0, qxre), (1, qxim)):
                        for i in range(4):
                            m_ = mbase + i
                            colblk = (m_ * 2 + ri) * 2
                            bank = PS_QX[i // 2]
                            off = CH * (i % 2)
                            mm(ps[bank][:, off:off + CH], selq_sb[:, 128 * colblk:128 * (colblk + 1)], qsb[:, 2 * kc, :],
                               start=True, stop=False,
                               waits=([("ac", tqev), ("ac", C["ac"])] if i == 0 else ()))
                            mm(ps[bank][:, off:off + CH], selq_sb[:, 128 * (colblk + 1):128 * (colblk + 2)], qsb[:, 2 * kc + 1, :],
                               start=False, stop=True, inc=("tn", 1))
                        tqx = C["tn"]
                        p.ev("scalar", (lambda e, qx=qx: e.activation(qx[:, 0:2, :].rearrange("q a v -> q (a v)"), ps[PS_QX[0]][:, 0:2 * CH], AF.Copy)),
                             waits=[("tn", tqx)], inc=("ac", 1))
                        p.ev("scalar", (lambda e, qx=qx: e.activation(qx[:, 2:4, :].rearrange("q a v -> q (a v)"), ps[PS_QX[1]][:, 0:2 * CH], AF.Copy)),
                             inc=("ac", 1))
                    tqxe = C["ac"]
                    if DBG and b == 0 and c == 0 and kc == 0 and mh == 0:
                        p.ev("gpsimd", (lambda e: e.dma_start(out=dbg["dbgQX"], in_=qxre[:])), waits=[("ac", tqxe)], inc=("gd", 16))
                    vop((lambda e: e.tensor_mul(trA[:], trD[:], qxre[:])),
                        waits=[("ac", tqxe), ("vg", C["vg"])])
                    vop(lambda e: e.tensor_mul(trC[:], trB[:], qxim[:]))
                    vop(lambda e: e.tensor_add(trA[:], trA[:], trC[:]))
                    vop(lambda e: e.tensor_mul(trC[:], trB[:], qxre[:]))
                    vop(lambda e: e.tensor_mul(trB[:], trD[:], qxim[:]))
                    vop(lambda e: e.tensor_tensor(trC[:], trC[:], trB[:], ALU.subtract))
                    vA = bass.AP(trA, 0, [[4 * CH, 128], [1, CH], [CH, 4]])
                    vC = bass.AP(trC, 0, [[4 * CH, 128], [1, CH], [CH, 4]])
                    if mh == 0:
                        vgop(lambda e, vA=vA, kc=kc: e.tensor_reduce(orsb[:, kc, :], vA, AX.X, ALU.add))
                        vgop(lambda e, vC=vC, kc=kc: e.tensor_reduce(oisb[:, kc, :], vC, AX.X, ALU.add))
                    else:
                        vgop(lambda e, vA=vA: e.tensor_reduce(trD[:, 0, :], vA, AX.X, ALU.add))
                        vgop(lambda e, vC=vC: e.tensor_reduce(trD[:, 1, :], vC, AX.X, ALU.add))
                        vgop(lambda e, kc=kc: e.tensor_add(orsb[:, kc, :], orsb[:, kc, :], trD[:, 0, :]))
                        vgop(lambda e, kc=kc: e.tensor_add(oisb[:, kc, :], oisb[:, kc, :], trD[:, 1, :]))
                idb = ps[PS_ID][:, CH * kc:CH * (kc + 1)]
                vgop(lambda e, kc=kc, idb=idb: e.tensor_tensor(orsb[:, kc, :], orsb[:, kc, :], idb, ALU.mult))
                vgop(lambda e, kc=kc, idb=idb: e.tensor_tensor(oisb[:, kc, :], oisb[:, kc, :], idb, ALU.mult))
                vgop(lambda e, kc=kc: e.tensor_copy(orb[:, kc, :], orsb[:, kc, :]))
                vgop(lambda e, kc=kc: e.tensor_copy(oib[:, kc, :], oisb[:, kc, :]))
            toi = C["vg"]
            if DBG and b == 0 and c == 0:
                p.ev("gpsimd", (lambda e: e.dma_start(out=dbg["dbgD"], in_=orsb[:])), waits=[("vg", toi)], inc=("gd", 16))

            if P2S < 6:
                continue
            ytok = {}
            for j in range(6):
                bv = PS_GEN[(2 * j) % 4]
                bg = PS_GEN[(2 * j + 1) % 4]
                yw = [("vg", toi), ("ac", C["ac"])] if j == 0 else []
                if j >= 2:
                    yw = yw + [("vg", ytok[j - 2][0]), ("ac", ytok[j - 2][1])]
                mm(ps[bv][:, 0:CH], rebd_b[:, 0, 128 * j:128 * (j + 1)], orb[:, 0, :], start=True, stop=False, waits=yw)
                mm(ps[bv][:, 0:CH], imbd_b[:, 0, 128 * j:128 * (j + 1)], oib[:, 0, :], start=False, stop=False)
                mm(ps[bv][:, 0:CH], rebd_b[:, 1, 128 * j:128 * (j + 1)], orb[:, 1, :], start=False, stop=False)
                mm(ps[bv][:, 0:CH], imbd_b[:, 1, 128 * j:128 * (j + 1)], oib[:, 1, :], start=False, stop=True, inc=("tn", 1))
                mm(ps[bg][:, 0:CH], rebd_b[:, 0, 768 + 128 * j:768 + 128 * (j + 1)], orb[:, 0, :], start=True, stop=False)
                mm(ps[bg][:, 0:CH], imbd_b[:, 0, 768 + 128 * j:768 + 128 * (j + 1)], oib[:, 0, :], start=False, stop=False)
                mm(ps[bg][:, 0:CH], rebd_b[:, 1, 768 + 128 * j:768 + 128 * (j + 1)], orb[:, 1, :], start=False, stop=False)
                ty = mm(ps[bg][:, 0:CH], imbd_b[:, 1, 768 + 128 * j:768 + 128 * (j + 1)], oib[:, 1, :], start=False, stop=True, inc=("tn", 1))
                tsi = p.ev("scalar", (lambda e, bg=bg: e.activation(silu_t[:], ps[bg][:, 0:CH], AF.Silu)),
                           waits=[("tn", ty), ("ac", C["ac"])], inc=("ac", 1))
                tyj = p.ev("vector", (lambda e, j=j, bv=bv: e.tensor_tensor(yact[:, j, :], ps[bv][:, 0:CH], silu_t[:], ALU.mult)),
                     waits=[("ac", tsi), ("vg", C["vg"])], inc=("vg", 1))
                ytok[j] = (tyj, tsi)
            tya = C["vg"]
            if DBG and b == 0 and c == 0:
                p.ev("gpsimd", (lambda e: e.dma_start(out=dbg["dbgE"], in_=yact[:])), waits=[("vg", tya)], inc=("gd", 16))

            if P2S < 7:
                continue
            for tt_ in range(2):
                r0 = row0 + 128 * tt_
                for dc in range(4):
                    for j in range(6):
                        mm(ps[PS_GEN[dc]][:, 0:512], yact[:, j, 128 * tt_:128 * (tt_ + 1)], wout_b[:, j, 512 * dc:512 * (dc + 1)],
                           start=(j == 0), stop=(j == 5),
                           waits=([("vg", tya), ("ac", C["ac"])] if (dc == 0 and j == 0) else ()),
                           inc=(("tn", 1) if j == 5 else None))
                tao = C["tn"]
                for dc in range(4):
                    p.ev("scalar", (lambda e, dc=dc: e.activation(asb[:, 512 * dc:512 * (dc + 1)], ps[PS_GEN[dc]][:, 0:512],
                                                                  AF.Copy, accum_out=stat[:, 8 + dc:9 + dc])),
                         waits=([("tn", tao), ("ac", C["ac"]), ("vg", C["vg"])] if dc == 0 else ()), inc=("ac", 1))
                tev = p.ev("scalar", (lambda e: e.activation(sqsc[:], asb[:], AF.Square, accum_out=stat2[:, 8:9])),
                           inc=("ac", 1))
                if DBG and b == 0 and c == 0 and tt_ == 0:
                    p.ev("gpsimd", (lambda e: e.dma_start(out=dbg["dbgF"], in_=asb[:])), waits=[("ac", tev)], inc=("gd", 16))
                vgop(lambda e: e.tensor_reduce(stat[:, 12:13], stat[:, 8:12], AX.X, ALU.add), waits=[("ac", tev)])
                vgop(lambda e: e.tensor_single_scalar(stat[:, 12:13], stat[:, 12:13], cst[:, 4:5], ALU.mult))
                vgop(lambda e: e.tensor_single_scalar(stat2[:, 8:9], stat2[:, 8:9], cst[:, 4:5], ALU.mult))
                vgop(lambda e: e.tensor_mul(nwt2[:, 0:1], stat[:, 12:13], stat[:, 12:13]))
                vgop(lambda e: e.tensor_tensor(stat2[:, 8:9], stat2[:, 8:9], nwt2[:, 0:1], ALU.subtract))
                newton_rsqrt(1, lambda: stat2[:, 8:9], lambda: rstd_sb[:, 4:5])
                tln2 = vgop(lambda e: e.tensor_mul(bias_sb[:, 4:5], stat[:, 12:13], rstd_sb[:, 4:5]))
                vop((lambda e, tt_=tt_: e.scalar_tensor_tensor(x1sb[:], asb[:], rstd_sb[:, 4:5], sx[:, tt_, :], ALU.mult, ALU.add)),
                    waits=[("vg", tln2), ("gd", C["gd"])])
                tx1 = vop(lambda e: e.tensor_single_scalar(x1sb[:], x1sb[:], bias_sb[:, 4:5], ALU.subtract))
                p.ev("gpsimd", (lambda e, r0=r0: e.dma_start(out=x1_d[r0:r0 + 128, :], in_=x1sb[:])),
                     waits=[("vg", tx1)], inc=("gd", 16))
                vgop(lambda e: e.tensor_reduce(stat[:, 13:14], x1sb[:], AX.X, ALU.add))
                tq3 = p.ev("scalar", (lambda e: e.activation(sqsc[:], x1sb[:], AF.Square, accum_out=stat2[:, 9:10])),
                           waits=[("vg", C["vg"]), ("ac", C["ac"])], inc=("ac", 1))
                vgop(lambda e: e.tensor_single_scalar(stat[:, 13:14], stat[:, 13:14], cst[:, 4:5], ALU.mult))
                vgop(lambda e: e.tensor_single_scalar(stat2[:, 9:10], stat2[:, 9:10], cst[:, 4:5], ALU.mult),
                     waits=[("ac", tq3)])
                vgop(lambda e: e.tensor_mul(nwt2[:, 0:1], stat[:, 13:14], stat[:, 13:14]))
                vgop(lambda e: e.tensor_tensor(stat2[:, 9:10], stat2[:, 9:10], nwt2[:, 0:1], ALU.subtract))
                newton_rsqrt(1, lambda: stat2[:, 9:10], lambda: rstd_sb[:, 5:6])
                vgop(lambda e: e.tensor_mul(bias_sb[:, 5:6], stat[:, 13:14], rstd_sb[:, 5:6]))
                tln3 = vgop(lambda e: e.tensor_scalar_mul(bias_sb[:, 5:6], bias_sb[:, 5:6], -1.0))
                tl3 = p.ev("scalar", (lambda e: e.activation(xl3st[:], x1sb[:], AF.Prelu, alpha=1.0, scale=rstd_sb[:, 5:6], bias=bias_sb[:, 5:6])),
                           waits=[("vg", tln3), ("gd", C["gd"])], inc=("ac", 1))
                p.ev("gpsimd", (lambda e, r0=r0: e.dma_start(out=xln3_d[r0:r0 + 128, :], in_=xl3st[:])),
                     waits=[("ac", tl3)], inc=("gd", 16))
            p.w("sync", [("vg", C["vg"]), ("ac", C["ac"]), ("tn", C["tn"])])
    t_attn_done = [("vg", C["vg"]), ("ac", C["ac"]), ("tn", C["tn"]), ("gd", C["gd"])]

    ck.close()
    p2.close()

    dqs = contextlib.ExitStack()
    ecd = dqs.enter_context
    dq_pk = ecd(nc.sbuf_tensor("dq_pk", [128, 2, 1024], I32))
    dq_sc = ecd(nc.sbuf_tensor("dq_sc", [128, 2, 64], F32))
    dq_bf = ecd(nc.sbuf_tensor("dq_bf", [128, 1024], F32))
    dq_hi = ecd(nc.sbuf_tensor("dq_hi", [128, 1024], I32))
    dq_hf = ecd(nc.sbuf_tensor("dq_hf", [128, 1024], F32))
    dq_c = ecd(nc.sbuf_tensor("dq_c", [128, 2048], BF16))
    dq_d1 = ecd(nc.sbuf_tensor("dq_d1", [128, 2048], BF16))
    dq_d2 = ecd(nc.sbuf_tensor("dq_d2", [128, 2048], BF16))
    dq_d3 = ecd(nc.sbuf_tensor("dq_d3", [128, 2048], BF16))
    dq_d4 = ecd(nc.sbuf_tensor("dq_d4", [128, 2048], BF16))
    dq_e = ecd(nc.sbuf_tensor("dq_e", [128, 2048], BF16))
    dq_sx = ecd(nc.sbuf_tensor("dq_sx", [128, 2048], F32))
    dq_o = ecd(nc.sbuf_tensor("dq_o", [128, 2, 2048], BF16))

    dq_free = {}
    dq_freeg = {}
    dq_store = {}
    dq_w11 = {}
    dq_w14 = {}
    dq_state = {"r": 0, "first": True}

    GQ, GS = (("gpsimd", "gp") if int(os.environ.get("K_GP", "0")) else ("vector", "vg"))

    def gop(fn, waits=()):
        return p.ev(GQ, fn, waits=[(GS, C[GS])] + list(waits), inc=(GS, 1))

    def emit_dec_tile(pk_in, sc_in, wdst, rr, cc):
        r = dq_state["r"]
        dq_state["r"] = r + 1
        bi = r % 2
        lw = [("vg", dq_free.get(r - 2, 0)), (GS, dq_freeg.get(r - 2, 0))]
        if dq_state["first"]:
            lw = lw + list(t_attn_done)
            dq_state["first"] = False
        tl = p.ev("gpsimd", (lambda e, rr=rr, cc=cc, bi=bi, pk_in=pk_in:
                             e.dma_start(out=dq_pk[:, bi, :], in_=pk_in[128 * rr:128 * (rr + 1), 1024 * cc:1024 * (cc + 1)])),
                  waits=lw, inc=("ld3", 16))
        tl2 = p.ev("gpsimd", (lambda e, rr=rr, cc=cc, bi=bi, sc_in=sc_in:
                              e.dma_start(out=dq_sc[:, bi, :], in_=sc_in[128 * rr:128 * (rr + 1), 64 * cc:64 * (cc + 1)])),
                   inc=("ld3", 16))
        scx = dq_sx[:].rearrange("q (v w) -> q v w", w=32)
        gop(lambda e, bi=bi, scx=scx: e.tensor_copy(scx[:, :, 0:1], dq_sc[:, bi, :].unsqueeze(2)),
            waits=[("ld3", tl2)])
        for w_ in (1, 2, 4, 8, 16):
            gop(lambda e, w_=w_, scx=scx: e.tensor_copy(scx[:, :, w_:2 * w_], scx[:, :, 0:w_]))
        dq_freeg[r] = C[GS]
        vop((lambda e, bi=bi: e.tensor_copy(dq_bf[:], dq_pk[:, bi, :])),
            waits=[("ld3", tl), ("vg", C["vg"])])
        vop(lambda e: e.tensor_scalar(dq_hi[:], dq_bf[:], 1.0 / 16.0, -0.46875, ALU.mult, ALU.add))
        vop(lambda e: e.tensor_copy(dq_hf[:], dq_hi[:]))
        vop(lambda e: e.scalar_tensor_tensor(dq_c[:, 0:2048:2], dq_hf[:], -16.0, dq_bf[:], ALU.mult, ALU.add))
        vop(lambda e: e.tensor_copy(dq_c[:, 1:2048:2], dq_hf[:]))
        v6 = vop((lambda e: e.tensor_single_scalar(dq_d1[:], dq_c[:], 8.0, ALU.is_ge)),
                 waits=[(GS, dq_w11.get(r - 1, 0))])
        tm_ = vop((lambda e: e.scalar_tensor_tensor(dq_d2[:], dq_d1[:], -8.0, dq_c[:], ALU.mult, ALU.add)),
                  waits=[(GS, dq_w14.get(r - 1, 0))])
        dq_w11[r] = gop(lambda e: e.tensor_scalar(dq_e[:], dq_d1[:], -2.0, 1.0, ALU.mult, ALU.add),
                        waits=[("vg", v6)])
        p.ev("scalar", (lambda e: e.activation(dq_d3[:], dq_d2[:], AF.Relu, bias=cst[:, 2:3])),
             waits=[("vg", tm_), ("ac", C["ac"])], inc=("ac", 1))
        ta_ = p.ev("scalar", (lambda e: e.activation(dq_d4[:], dq_d2[:], AF.Relu, bias=cst[:, 3:4])),
                   inc=("ac", 1))
        vop((lambda e: e.tensor_add(dq_d2[:], dq_d2[:], dq_d3[:])), waits=[("ac", ta_)])
        v10 = vop(lambda e: e.scalar_tensor_tensor(dq_d2[:], dq_d2[:], 0.5, dq_d4[:], ALU.mult, ALU.add))
        dq_free[r] = v10
        dq_w14[r] = gop(lambda e: e.tensor_mul(dq_e[:], dq_e[:], dq_d2[:]), waits=[("vg", v10)])
        t14 = gop((lambda e, bi=bi: e.tensor_tensor(dq_o[:, bi, :], dq_e[:], dq_sx[:], ALU.mult)),
                  waits=[("gd", dq_store.get(r - 2, 0))])
        dq_store[r] = p.ev("gpsimd", (lambda e, rr=rr, cc=cc, bi=bi, wdst=wdst:
                                      e.dma_start(out=wdst[128 * rr:128 * (rr + 1), 2048 * cc:2048 * (cc + 1)], in_=dq_o[:, bi, :])),
                           waits=[(GS, t14)], inc=("gd", 16))

    p3 = contextlib.ExitStack()
    ec3 = p3.enter_context
    TH = T // 2
    xfm3 = ec3(nc.sbuf_tensor("xfm3", [128, KD, TH], BF16))
    wgt = [ec3(nc.sbuf_tensor(f"wgt{i}", [128, KD, 256], BF16)) for i in range(2)]
    wut = [ec3(nc.sbuf_tensor(f"wut{i}", [128, KD, 256], BF16)) for i in range(2)]
    gl1 = ec3(nc.sbuf_tensor("gl1", [128, 512], F32))
    gl2 = ec3(nc.sbuf_tensor("gl2", [128, 512], F32))
    gl3 = ec3(nc.sbuf_tensor("gl3", [128, 512], F32))
    ibuf = [ec3(nc.sbuf_tensor(f"ibuf{i}", [128, 512], BF16)) for i in range(2)]

    NG = (NMLP // 256) if upto >= 3 else 0
    dec_gu_gd = {}
    dn_gd = {}
    dn_tiles = [(rr, cc) for cc in range(NMLP // 2048) for rr in range(D // 128)]
    mm_tok = {}
    ev_tok = {}
    st_tok = {}

    def emit_dec_group(g):
        for pk_in, sc_in, wdst in ((gpk_in, gsc_in, wg_d), (upk_in, usc_in, wu_d)):
            for j in (0, 1):
                emit_dec_tile(pk_in, sc_in, wdst, 2 * g + j, 0)
        dec_gu_gd[g] = C["gd"]

    for g0 in range(min(4, NG)):
        emit_dec_group(g0)
    for half in range(2 if upto >= 3 else 0):
        t0h = TH * half
        xw = list(t_attn_done) if half == 0 else [("tn", C["tn"])]
        for k in range(KD):
            for t4 in range(4):
                p.ev("sync", (lambda e, k=k, t4=t4, t0h=t0h: e.dma_start_transpose(
                    xfm3[:, k, 512 * t4:512 * (t4 + 1)], xln3_d[t0h + 512 * t4:t0h + 512 * (t4 + 1), 128 * k:128 * (k + 1)])),
                    waits=(xw if (k == 0 and t4 == 0) else ()), inc=("ld2", 16))
        txf3 = C["ld2"]
        for ng in range(NG):
            gi = half * NG + ng
            bi = gi % 2
            wl_waits = [("ld2", txf3)]
            if half == 0:
                wl_waits.append(("gd", dec_gu_gd[ng]))
            if gi >= 2:
                wl_waits.append(("tn", mm_tok[gi - 2]))
            for k in range(KD):
                p.ev("sync", (lambda e, ng=ng, bi=bi, k=k: e.dma_start_transpose(
                    wgt[bi][:, k, :], wg_d[256 * ng:256 * (ng + 1), 128 * k:128 * (k + 1)])),
                    waits=(wl_waits if k == 0 else ()), inc=("ld2", 16))
            for k in range(KD):
                p.ev("sync", (lambda e, ng=ng, bi=bi, k=k: e.dma_start_transpose(
                    wut[bi][:, k, :], wu_d[256 * ng:256 * (ng + 1), 128 * k:128 * (k + 1)])),
                    inc=("ld2", 16))
            twl = C["ld2"]
            for ns in range(2):
                for t4 in range(4):
                    it = (gi * 2 + ns) * 4 + t4
                    bgk = it % 4
                    buk = 4 + it % 4
                    mmw = []
                    if ns == 0 and t4 == 0:
                        mmw.append(("ld2", twl))
                    if it >= 4:
                        mmw.append(("vg", ev_tok.get(it - 4, 0)))
                    for k in range(KD):
                        mm(ps[bgk][:, 0:512], wgt[bi][:, k, 128 * ns:128 * ns + 128], xfm3[:, k, 512 * t4:512 * (t4 + 1)],
                           start=(k == 0), stop=(k == KD - 1), waits=(mmw if k == 0 else ()))
                    for k in range(KD):
                        mm(ps[buk][:, 0:512], wut[bi][:, k, 128 * ns:128 * ns + 128], xfm3[:, k, 512 * t4:512 * (t4 + 1)],
                           start=(k == 0), stop=(k == KD - 1), inc=(("tn", 1) if k == KD - 1 else None))
                    tmm = C["tn"]
                    if ns == 1 and t4 == 3:
                        mm_tok[gi] = tmm
                    tg1 = p.ev("scalar", (lambda e, bgk=bgk: e.activation(gl1[:], ps[bgk][:, 0:512], AF.Square)),
                               waits=[("tn", tmm), ("ac", C["ac"]), ("vg", C["vg"])], inc=("ac", 1))
                    vop((lambda e: e.tensor_scalar(gl2[:], gl1[:], GELU_C3, GELU_C1, ALU.mult, ALU.add)),
                        waits=[("ac", tg1), ("vg", C["vg"])])
                    tg2 = vop(lambda e, bgk=bgk: e.tensor_tensor(gl2[:], gl2[:], ps[bgk][:, 0:512], ALU.mult))
                    tg3 = p.ev("scalar", (lambda e: e.activation(gl3[:], gl2[:], AF.Tanh)),
                               waits=[("vg", tg2)], inc=("ac", 1))
                    vop((lambda e: e.tensor_scalar(gl3[:], gl3[:], 0.5, 0.5, ALU.mult, ALU.add)),
                        waits=[("ac", tg3), ("vg", C["vg"])])
                    vop(lambda e, bgk=bgk: e.tensor_tensor(gl3[:], gl3[:], ps[bgk][:, 0:512], ALU.mult))
                    ibb = ibuf[it % 2]
                    tev3 = vop((lambda e, buk=buk, ibb=ibb: e.tensor_tensor(ibb[:], gl3[:], ps[buk][:, 0:512], ALU.mult)),
                               waits=[("gd2", st_tok.get(it - 2, 0))])
                    ev_tok[it] = tev3
                    n0 = 256 * ng + 128 * ns
                    t0 = t0h + 512 * t4
                    st_tok[it] = p.ev("gpsimd", (lambda e, n0=n0, t0=t0, ibb=ibb: e.dma_start(
                        out=inter_d[n0:n0 + 128, t0:t0 + 512], in_=ibb[:])),
                        waits=[("vg", tev3)], inc=("gd2", 16))
            if half == 0 and ng + 4 < NG:
                emit_dec_group(ng + 4)
            if half == 1 and 2 * ng < len(dn_tiles):
                for j in (0, 1):
                    rr, cc = dn_tiles[2 * ng + j]
                    emit_dec_tile(dpk_in, dsc_in, wd_d, rr, cc)
                    dn_gd[2 * ng + j] = C["gd"]
    t_p3 = [("vg", C["vg"]), ("ac", C["ac"]), ("tn", C["tn"]), ("gd2", C["gd2"]), ("gd", C["gd"]), ("gp", C["gp"])]

    p3.close()
    dqs.close()
    p4 = contextlib.ExitStack()
    ec4 = p4.enter_context
    isb = ec4(nc.sbuf_tensor("isb", [128, KD, T], BF16))
    wdt = [ec4(nc.sbuf_tensor(f"wdt{i}", [128, KD, 512], BF16)) for i in range(2)]
    dsb = [ec4(nc.sbuf_tensor(f"dsb{i}", [128, 512], F32)) for i in range(2)]

    d_ev = {}
    d_st = {}
    acc_prev = 0
    for nm in range(min(P4NM, 8) if upto >= 4 else 0):
        il_w = list(t_p3) if nm == 0 else [("tn", C["tn"])]
        for k in range(KD):
            p.ev("sync", (lambda e, nm=nm, k=k: e.dma_start(
                out=isb[:, k, :], in_=inter_d[2048 * nm + 128 * k:2048 * nm + 128 * (k + 1), :])),
                waits=(il_w if k == 0 else ()), inc=("ld2", 16))
        til = C["ld2"]
        for hb in range(4):
            bi = hb % 2
            w4 = [("gd", dn_gd.get(nm * 16 + 4 * hb + 3, 0))]
            if nm > 0 or hb >= 2:
                w4.append(("tn", C["tn"]))
            for k in range(KD):
                p.ev("sync", (lambda e, nm=nm, hb=hb, bi=bi, k=k: e.dma_start_transpose(
                    wdt[bi][:, k, :], wd_d[512 * hb:512 * (hb + 1), 2048 * nm + 128 * k:2048 * nm + 128 * (k + 1)])),
                    waits=(w4 if k == 0 else ()), inc=("ld2", 16))
            twd = C["ld2"]
            for t32 in range(32):
                git = nm * 128 + hb * 32 + t32
                bank = PS_GEN[git % 4]
                mmw = []
                if t32 == 0:
                    mmw += [("ld2", til), ("ld2", twd)]
                if git >= 4:
                    mmw.append(("vg", d_ev.get(git - 4, 0)))
                for k in range(KD):
                    mm(ps[bank][:, 0:512], isb[:, k, 128 * t32:128 * (t32 + 1)], wdt[bi][:, k, :],
                       start=(k == 0), stop=(k == KD - 1), waits=(mmw if k == 0 else ()),
                       inc=(("tn", 1) if k == KD - 1 else None))
                tmm = C["tn"]
                db = dsb[git % 2]
                tev4 = p.ev("vector", (lambda e, bank=bank, db=db: e.tensor_copy(db[:], ps[bank][:, 0:512])),
                            waits=[("tn", tmm), ("vg", C["vg"]), ("gd2", d_st.get(git - 2, 0))], inc=("vg", 1))
                d_ev[git] = tev4
                accw = [("vg", tev4)]
                if nm > 0:
                    accw.append(("gd2", acc_prev))
                if nm == 0:
                    d_st[git] = p.ev("gpsimd", (lambda e, hb=hb, t32=t32, db=db: e.dma_start(
                        out=out2_d[128 * t32:128 * (t32 + 1), 512 * hb:512 * (hb + 1)], in_=db[:])),
                        waits=accw, inc=("gd2", 16))
                else:
                    d_st[git] = p.ev("gpsimd", (lambda e, hb=hb, t32=t32, db=db: e.dma_start(
                        out=out2_d[128 * t32:128 * (t32 + 1), 512 * hb:512 * (hb + 1)], in_=db[:],
                        accum_op=ALU.add)),
                        waits=accw, inc=("gd2", 16))
        acc_prev = C["gd2"]
    t_p4_gd = C["gd2"]

    p4.close()
    p5 = contextlib.ExitStack()
    ec5 = p5.enter_context
    o2sb = ec5(nc.sbuf_tensor("o2sb", [128, D], F32))
    x1rb = ec5(nc.sbuf_tensor("x1rb", [128, D], F32))
    sq5 = ec5(nc.sbuf_tensor("sq5", [128, D], F32))
    outsb = ec5(nc.sbuf_tensor("outsb", [128, D], F32))

    for t32 in range(32 if upto >= 5 else 0):
        r0 = 128 * t32
        lw5 = [("vg", C["vg"]), ("ac", C["ac"])]
        if t32 == 0:
            lw5.append(("gd2", t_p4_gd))
        tl = p.ev("sync", (lambda e, r0=r0: e.dma_start(out=o2sb[:], in_=out2_d[r0:r0 + 128, :])),
                  waits=lw5, inc=("ld2", 16))
        tl2 = p.ev("sync", (lambda e, r0=r0: e.dma_start(out=x1rb[:], in_=x1_d[r0:r0 + 128, :])), inc=("ld2", 16))
        p.ev("vector", (lambda e: e.tensor_reduce(stat[:, 14:15], o2sb[:], AX.X, ALU.add)),
             waits=[("ld2", tl), ("vg", C["vg"])], inc=("vg", 1))
        tq5 = p.ev("scalar", (lambda e: e.activation(sq5[:], o2sb[:], AF.Square, accum_out=stat2[:, 10:11])),
                   waits=[("ld2", tl), ("ac", C["ac"])], inc=("ac", 1))
        vgop(lambda e: e.tensor_single_scalar(stat[:, 14:15], stat[:, 14:15], cst[:, 4:5], ALU.mult))
        vgop(lambda e: e.tensor_single_scalar(stat2[:, 10:11], stat2[:, 10:11], cst[:, 4:5], ALU.mult),
             waits=[("ac", tq5)])
        vgop(lambda e: e.tensor_mul(nwt2[:, 0:1], stat[:, 14:15], stat[:, 14:15]))
        vgop(lambda e: e.tensor_tensor(stat2[:, 10:11], stat2[:, 10:11], nwt2[:, 0:1], ALU.subtract))
        newton_rsqrt(1, lambda: stat2[:, 10:11], lambda: rstd_sb[:, 6:7])
        tln4 = vgop(lambda e: e.tensor_mul(bias_sb[:, 6:7], stat[:, 14:15], rstd_sb[:, 6:7]))
        vop((lambda e: e.scalar_tensor_tensor(outsb[:], o2sb[:], rstd_sb[:, 6:7], x1rb[:], ALU.mult, ALU.add)),
            waits=[("vg", tln4), ("ld2", tl2), ("gd2", C["gd2"])])
        tout = vop(lambda e: e.tensor_single_scalar(outsb[:], outsb[:], bias_sb[:, 6:7], ALU.subtract))
        p.ev("gpsimd", (lambda e, r0=r0: e.dma_start(out=out_d[r0:r0 + 128, :], in_=outsb[:])),
             waits=[("vg", tout)], inc=("gd2", 16))
    if DBG:
        p.ev("gpsimd", (lambda e: e.dma_start(out=dbg["dbgG"], in_=xln1_d[0:256, :])), waits=[("gd2", C["gd2"]), ("gd", C["gd"]), ("vg", C["vg"])], inc=("gd2", 16))
        p.ev("gpsimd", (lambda e: e.dma_start(out=dbg["dbgH"], in_=x1_d[0:256, :])), inc=("gd2", 16))
        p.ev("gpsimd", (lambda e: e.dma_start(out=dbg["dbgI"], in_=wg_d[0:128, :])), inc=("gd2", 16))
        p.ev("gpsimd", (lambda e: e.dma_start(out=dbg["dbgJ"], in_=inter_d[0:128, :])), inc=("gd2", 16))
        p.ev("gpsimd", (lambda e: e.dma_start(out=dbg["dbgK"], in_=out2_d[0:256, :])), inc=("gd2", 16))
    p.w("sync", [("gd2", C["gd2"])])
    p.ev("sync", lambda e: e.nop())

    sems = {}
    with contextlib.ExitStack() as semstack:
        for s in sorted(p.sem_names):
            sems[s] = semstack.enter_context(nc.semaphore(s))
        with nc.Block() as block:
            @block.sync
            def _(eng):
                p.replay("sync", eng, sems)

            @block.tensor
            def _(eng):
                p.replay("tensor", eng, sems)

            @block.vector
            def _(eng):
                p.replay("vector", eng, sems)

            @block.scalar
            def _(eng):
                p.replay("scalar", eng, sems)

            @block.gpsimd
            def _(eng):
                p.replay("gpsimd", eng, sems)
    p5.close()
    stack.close()
    return nc


def _host_layout(inputs):
    att_x = np.ascontiguousarray(inputs["L0_attn_x"])
    W_mem = np.ascontiguousarray(inputs["L0_attn_W_mem"], dtype=np.float32)
    conv_k = np.asarray(inputs["L0_attn_conv_k"], dtype=np.float32)
    W_q = np.ascontiguousarray(inputs["L0_attn_W_q"], dtype=np.float32)
    theta = np.asarray(inputs["L0_attn_theta"], dtype=np.float32).reshape(32, 8, 8)
    dec = np.asarray(inputs["L0_attn_decay_slopes"], dtype=np.float32)
    anc = np.asarray(inputs["L0_attn_anchor_slopes"], dtype=np.float32)
    score = np.asarray(inputs["L0_attn_score_scale"], dtype=np.float32)
    W_re = np.asarray(inputs["L0_attn_W_re"], dtype=np.float32)
    W_im = np.asarray(inputs["L0_attn_W_im"], dtype=np.float32)
    nsc = np.asarray(inputs["L0_attn_norm_scale"], dtype=np.float32).reshape(256, 1)
    W_out = np.ascontiguousarray(inputs["L0_attn_W_out"], dtype=np.float32)

    convkt = np.zeros((384, 4), np.float32)
    convkt[0:288, :] = conv_k[:, 0, :].T
    theta_t = np.ascontiguousarray(theta.reshape(256, 8))
    slopes = np.concatenate([dec, anc]).reshape(32, 1).astype(np.float32)
    score_c = score.reshape(32, 1).astype(np.float32)

    pos = np.arange(L, dtype=np.float32)
    posdist = np.empty((32, L), np.float32)
    posdist[0:28, :] = (L - 1) - pos[None, :]
    posdist[28:32, :] = pos[None, :]

    def blockdiag(W):
        out = np.zeros((256, 1536), np.float32)
        for k in range(32):
            for h in range(8):
                out[k * 8 + h, 24 * k:24 * (k + 1)] = W[k, h, 0:24]
                out[k * 8 + h, 768 + 24 * k:768 + 24 * (k + 1)] = W[k, h, 24:48]
        return out
    rebd = blockdiag(W_re)
    imbd = blockdiag(W_im)

    selpk = np.zeros((32, 2, 128), np.float32)
    for cc in range(2):
        for i in range(128):
            selpk[16 * cc + i // 8, cc, i] = 1.0
    selq = np.zeros((128, 8, 2, 2, 128), np.float32)
    for m in range(8):
        for ri in range(2):
            for g in range(2):
                for i in range(128):
                    if (i // 64) == g:
                        selq[(i % 8) * 16 + 2 * m + ri, m, ri, g, i] = 1.0
    selq = np.ascontiguousarray(selq.reshape(128, 4096))

    common = {
        "wmem": W_mem, "convkt": convkt, "wq": W_q, "theta": theta_t,
        "slopes": slopes, "score": score_c, "rebd": rebd, "imbd": imbd,
        "nsc": nsc, "wout": W_out, "posdist": posdist,
        "selpk": selpk, "selq": selq,
        "gpk": np.ascontiguousarray(inputs["L0_ffn_gate_W_packed"]),
        "gsc": np.ascontiguousarray(inputs["L0_ffn_gate_scales"], dtype=np.float32),
        "upk": np.ascontiguousarray(inputs["L0_ffn_up_W_packed"]),
        "usc": np.ascontiguousarray(inputs["L0_ffn_up_scales"], dtype=np.float32),
        "dpk": np.ascontiguousarray(inputs["L0_ffn_down_W_packed"]),
        "dsc": np.ascontiguousarray(inputs["L0_ffn_down_scales"], dtype=np.float32),
    }
    in_maps = []
    for core in range(N_CORES):
        m = dict(common)
        m["x"] = np.ascontiguousarray(att_x[B_PER * core:B_PER * (core + 1)].reshape(T, D), dtype=np.float32)
        in_maps.append(m)
    return in_maps


def kernel(**inputs):
    in_maps = _host_layout(inputs)
    nc = build_program()
    res = run_bass_kernel_spmd(nc, in_maps, core_ids=list(range(N_CORES)))
    out = np.concatenate([r["out"].reshape(B_PER, L, D) for r in res.results], axis=0)
    return out.astype(np.float32)


# version 08
# speedup vs baseline: 1.1126x; 1.1126x over previous
import numpy as np
import ml_dtypes
import os
import contextlib
from collections import defaultdict

import concourse.bass as bass
import concourse.mybir as mybir
from concourse.bass_utils import run_bass_kernel_spmd

F32 = mybir.dt.float32
BF16 = mybir.dt.bfloat16
I32 = mybir.dt.int32
AF = mybir.ActivationFunctionType
ALU = mybir.AluOpType
AX = mybir.AxisListType
BF = ml_dtypes.bfloat16

PI = float(np.pi)
INV2PI = float(1.0 / (2 * np.pi))
N2PI = float(-2 * np.pi)

N_CORES = 8
B_PER = 2
L = 2048
D = 2048
T = B_PER * L
CH = 256
NCH = L // CH
KD = D // 128
MEM = 288
NQ = 512
NMLP = 16384
GELU_C1 = 0.7978845608028654
GELU_C3 = float(0.7978845608028654 * 0.044715)

ENGS = ("sync", "tensor", "vector", "scalar", "gpsimd")
P2S = int(os.environ.get("K_P2S", "9"))
DBG = int(os.environ.get("K_DBG", "0"))
P4NM = int(os.environ.get("K_P4NM", "8"))


class Prog:
    def __init__(self):
        self.q = {e: [] for e in ENGS}
        self.C = defaultdict(int)
        self.sem_names = set()

    def ev(self, e, fn, waits=(), inc=None):
        if inc is not None:
            self.sem_names.add(inc[0])
            self.C[inc[0]] += inc[1]
            tok = self.C[inc[0]]
        else:
            tok = None
        for s, v in waits:
            self.sem_names.add(s)
        self.q[e].append((fn, list(waits), inc))
        return tok

    def w(self, e, waits):
        for s, v in waits:
            self.sem_names.add(s)
        self.q[e].append((None, list(waits), None))

    def replay(self, e, eng, sems):
        pend = []
        for fn, waits, inc in self.q[e]:
            pend.extend(waits)
            if fn is None:
                continue
            for s, v in pend:
                eng.wait_ge(sems[s], v)
            pend = []
            r = fn(eng)
            if inc is not None:
                r.then_inc(sems[inc[0]], inc[1])
        assert not pend, f"dangling waits on {e}"


def build_program(upto=9):
    nc = bass.Bass("TRN2", target_bir_lowering=False)
    p = Prog()
    C = p.C

    def din(name, shape, dt):
        return nc.dram_tensor(name, shape, dt, kind="ExternalInput").ap()

    x_in = din("x", [T, D], F32)
    wmem_in = din("wmem", [D, MEM], F32)
    convkt_in = din("convkt", [384, 4], F32)
    wq_in = din("wq", [D, NQ], F32)
    theta_in = din("theta", [256, 8], F32)
    slopes_in = din("slopes", [32, 1], F32)
    score_in = din("score", [32, 1], F32)
    rebd_in = din("rebd", [256, 1536], F32)
    imbd_in = din("imbd", [256, 1536], F32)
    ns_in = din("nsc", [256, 1], F32)
    wout_in = din("wout", [768, D], F32)
    posdist_in = din("posdist", [32, L], F32)
    selpk_in = din("selpk", [32, 2, 128], F32)
    selq_in = din("selq", [128, 4096], F32)
    gpk_in = din("gpk", [NMLP, D // 2], I32)
    gsc_in = din("gsc", [NMLP, D // 32], F32)
    upk_in = din("upk", [NMLP, D // 2], I32)
    usc_in = din("usc", [NMLP, D // 32], F32)
    dpk_in = din("dpk", [D, NMLP // 2], I32)
    dsc_in = din("dsc", [D, NMLP // 32], F32)

    out_d = nc.dram_tensor("out", [T, D], F32, kind="ExternalOutput").ap()
    dbg = {}
    if DBG:
        for nm, shape, dt in (("dbgA", [128, 3, CH], F32), ("dbgB", [128, 6, CH], F32),
                              ("dbgC", [128, 4, CH], F32), ("dbgC2", [128, 4, CH], F32),
                              ("dbgD", [128, 2, CH], F32), ("dbgE", [128, 6, CH], F32),
                              ("dbgF", [128, D], F32), ("dbgQ", [128, 4, CH], F32),
                              ("dbgQX", [128, 4, CH], F32), ("dbgG", [256, D], BF16),
                              ("dbgH", [256, D], F32), ("dbgI", [128, D], BF16),
                              ("dbgJ", [128, T], BF16), ("dbgK", [256, D], F32)):
            dbg[nm] = nc.dram_tensor(nm, shape, dt, kind="ExternalOutput").ap()

    def dint(name, shape, dt):
        return nc.dram_tensor(name, shape, dt, kind="Internal").ap()

    wg_d = dint("wg_d", [NMLP, D], BF16)
    wu_d = dint("wu_d", [NMLP, D], BF16)
    wd_d = dint("wd_d", [D, NMLP], BF16)
    xln1_d = dint("xln1_d", [T, D], BF16)
    xln3_d = dint("xln3_d", [T, D], BF16)
    x1_d = dint("x1_d", [T, D], F32)
    inter_d = dint("inter_d", [NMLP, T], BF16)
    out2_d = dint("out2_d", [T, D], F32)

    stack = contextlib.ExitStack()
    ec = stack.enter_context

    cst = ec(nc.sbuf_tensor("cst", [128, 12], F32))
    imagic = ec(nc.sbuf_tensor("imagic", [128, 1], I32))
    theta_sb = ec(nc.sbuf_tensor("theta_sb", [128, 2, 8], F32))
    convkt_sb = ec(nc.sbuf_tensor("convkt_sb", [128, 3, 4], F32))
    slopes_sb = ec(nc.sbuf_tensor("slopes_sb", [128, 1], F32))
    score_sb = ec(nc.sbuf_tensor("score_sb", [128, 1], F32))
    sp_sb = ec(nc.sbuf_tensor("sp_sb", [128, 1], F32))
    tiny = ec(nc.sbuf_tensor("tiny", [128, 8], F32))
    logw_sb = ec(nc.sbuf_tensor("logw_sb", [128, L], F32))
    selpk_sb = ec(nc.sbuf_tensor("selpk_sb", [32, 2, 128], F32))
    stat = ec(nc.sbuf_tensor("stat", [128, 16], F32))
    stat2 = ec(nc.sbuf_tensor("stat2", [128, 16], F32))
    rstd_sb = ec(nc.sbuf_tensor("rstd_sb", [128, 8], F32))
    bias_sb = ec(nc.sbuf_tensor("bias_sb", [128, 8], F32))
    nwt = ec(nc.sbuf_tensor("nwt", [128, 8], F32))
    nwt2 = ec(nc.sbuf_tensor("nwt2", [128, 8], F32))
    nwi = ec(nc.sbuf_tensor("nwi", [128, 8], I32))
    carry_re = ec(nc.sbuf_tensor("carry_re", [128, 16], F32))
    carry_im = ec(nc.sbuf_tensor("carry_im", [128, 16], F32))
    carry_den = ec(nc.sbuf_tensor("carry_den", [128, 1], F32))

    ps = [ec(nc.psum_tensor(f"ps{i}", [128, 512], F32)) for i in range(8)]
    PS_GEN = (0, 1, 2, 3)
    PS_PW = 4
    PS_ID = 5
    PS_QX = (6, 7)

    def vop(fn, waits=()):
        return p.ev("vector", fn, waits=list(waits), inc=("vg", 1))

    def vgop(fn, waits=()):
        return p.ev("vector", fn, waits=[("vg", C["vg"])] + list(waits), inc=("vg", 1))

    def newton_rsqrt(n, var_ap_fn, out_ap_fn):
        vgop(lambda e: e.tensor_single_scalar(nwt[:, 0:n], var_ap_fn(), cst[:, 5:6], ALU.add))
        vgop(lambda e: e.tensor_copy(nwi[:, 0:n], nwt[:, 0:n].bitcast(I32)))
        vgop(lambda e: e.tensor_single_scalar(nwi[:, 0:n], nwi[:, 0:n], 1, ALU.logical_shift_right))
        vgop(lambda e: e.tensor_tensor(nwi[:, 0:n], imagic[:].to_broadcast([128, n]), nwi[:, 0:n], ALU.subtract))
        vgop(lambda e: e.tensor_copy(out_ap_fn(), nwi[:, 0:n].bitcast(F32)))
        for _ in range(4):
            vgop(lambda e: e.tensor_mul(nwt2[:, 0:n], out_ap_fn(), out_ap_fn()))
            vgop(lambda e: e.tensor_mul(nwt2[:, 0:n], nwt2[:, 0:n], nwt[:, 0:n]))
            vgop(lambda e: e.tensor_scalar(nwt2[:, 0:n], nwt2[:, 0:n], -0.5, 1.5, ALU.mult, ALU.add))
            vgop(lambda e: e.tensor_mul(out_ap_fn(), out_ap_fn(), nwt2[:, 0:n]))

    def mm(bank_ap, lhsT, rhs, start, stop, inc=None, waits=()):
        return p.ev("tensor", (lambda e, bank_ap=bank_ap, lhsT=lhsT, rhs=rhs, start=start, stop=stop:
                               e.matmul(bank_ap, lhsT, rhs, start=start, stop=stop)),
                    waits=waits, inc=inc)

    p2 = contextlib.ExitStack()
    ec2 = p2.enter_context
    wmem_b = ec2(nc.sbuf_tensor("wmem_b", [128, KD, MEM], BF16))
    wq_b = ec2(nc.sbuf_tensor("wq_b", [128, KD, NQ], BF16))
    wout_b = ec2(nc.sbuf_tensor("wout_b", [128, 6, D], BF16))
    rebd_b = ec2(nc.sbuf_tensor("rebd_b", [128, 2, 1536], BF16))
    imbd_b = ec2(nc.sbuf_tensor("imbd_b", [128, 2, 1536], BF16))
    selq_sb = ec2(nc.sbuf_tensor("selq_sb", [128, 4096], F32))

    p1s = contextlib.ExitStack()
    ec1 = p1s.enter_context
    sqsc0 = ec1(nc.sbuf_tensor("sqsc0", [128, D], F32))

    for k in range(KD):
        t = p.ev("sync", (lambda e, k=k: e.dma_start(out=sqsc0[:, 0:MEM], in_=wmem_in[128 * k:128 * (k + 1), :])), inc=("ld", 16))
        tv = p.ev("vector", (lambda e, k=k: e.tensor_copy(wmem_b[:, k, :], sqsc0[:, 0:MEM])),
                  waits=[("ld", t)], inc=("vg", 1))
        p.w("sync", [("vg", tv)])
    for k in range(KD):
        t = p.ev("sync", (lambda e, k=k: e.dma_start(out=sqsc0[:, 0:NQ], in_=wq_in[128 * k:128 * (k + 1), :])), inc=("ld", 16))
        tv = p.ev("vector", (lambda e, k=k: e.tensor_copy(wq_b[:, k, :], sqsc0[:, 0:NQ])),
                  waits=[("ld", t)], inc=("vg", 1))
        p.w("sync", [("vg", tv)])
    for g in range(6):
        t = p.ev("sync", (lambda e, g=g: e.dma_start(out=sqsc0[:, 0:D], in_=wout_in[128 * g:128 * (g + 1), :])), inc=("ld", 16))
        tv = p.ev("vector", (lambda e, g=g: e.tensor_copy(wout_b[:, g, :], sqsc0[:, 0:D])),
                  waits=[("ld", t)], inc=("vg", 1))
        p.w("sync", [("vg", tv)])
    t_ns = p.ev("sync", lambda e: e.dma_start(out=tiny[:, 0:2], in_=ns_in.rearrange("(g q) o -> q (g o)", q=128), allow_slow_non_contiguous=True), inc=("ld", 16))
    for src, dst in ((rebd_in, rebd_b), (imbd_in, imbd_b)):
        for g in range(2):
            t = p.ev("sync", (lambda e, src=src, g=g: e.dma_start(out=sqsc0[:, 0:1536], in_=src[128 * g:128 * (g + 1), :])), inc=("ld", 16))
            vgop((lambda e, g=g: e.tensor_single_scalar(sqsc0[:, 0:1536], sqsc0[:, 0:1536], tiny[:, g:g + 1], ALU.mult)),
                 waits=[("ld", t), ("ld", t_ns)])
            tv = vgop(lambda e, dst=dst, g=g: e.tensor_copy(dst[:, g, :], sqsc0[:, 0:1536]))
            p.w("sync", [("vg", tv)])
    p.ev("sync", lambda e: e.dma_start(out=theta_sb[:], in_=theta_in.rearrange("(g q) m -> q g m", q=128)), inc=("ld", 16))
    p.ev("sync", lambda e: e.dma_start(out=convkt_sb[:], in_=convkt_in.rearrange("(g q) w -> q g w", q=128)), inc=("ld", 16))
    p.ev("sync", lambda e: e.dma_start(out=slopes_sb[0:32, :], in_=slopes_in), inc=("ld", 16))
    p.ev("sync", lambda e: e.dma_start(out=score_sb[0:32, :], in_=score_in), inc=("ld", 16))
    p.ev("sync", lambda e: e.dma_start(out=selpk_sb[:], in_=selpk_in), inc=("ld", 16))
    p.ev("sync", lambda e: e.dma_start(out=selq_sb[:], in_=selq_in), inc=("ld", 16))
    t_pos = p.ev("sync", lambda e: e.dma_start(out=logw_sb[0:32, :], in_=posdist_in), inc=("ld", 16))
    tld_all = C["ld"]

    for col, val in ((0, PI / 2), (1, 0.5), (2, -4.0), (3, -6.0), (4, 1.0 / 2048.0),
                     (5, 1e-5), (7, 1e-4), (10, -1.0), (11, 1.0)):
        vgop(lambda e, col=col, val=val: e.memset(cst[:, col:col + 1], float(val)))
    vgop(lambda e: e.memset(imagic[:], 0x5F3759DF))

    vgop(lambda e: e.tensor_scalar_mul(tiny[:, 2:3], slopes_sb[:, 0:1], -1.0), waits=[("ld", tld_all)])
    vgop(lambda e: e.tensor_max(tiny[:, 3:4], slopes_sb[:, 0:1], tiny[:, 2:3]))
    vgop(lambda e: e.tensor_scalar_max(tiny[:, 4:5], slopes_sb[:, 0:1], 0.0))
    tsp = C["vg"]
    p.ev("scalar", lambda e: e.preload_act_table(AF.Exp), inc=("ac", 1))
    p.ev("scalar", lambda e: e.activation(tiny[:, 5:6], tiny[:, 3:4], AF.Exp, scale=cst[:, 10:11]),
         waits=[("vg", tsp)], inc=("ac", 1))
    p.ev("scalar", lambda e: e.preload_act_table(AF.Ln), inc=("ac", 1))
    p.ev("scalar", lambda e: e.activation(tiny[:, 6:7], tiny[:, 5:6], AF.Ln, bias=cst[:, 11:12]), inc=("ac", 1))
    tspa = C["ac"]
    vgop(lambda e: e.tensor_add(sp_sb[:, 0:1], tiny[:, 4:5], tiny[:, 6:7]), waits=[("ac", tspa)])
    vgop(lambda e: e.tensor_scalar(logw_sb[0:32, :], logw_sb[0:32, :], sp_sb[0:32, 0:1], -1.0, ALU.mult, ALU.mult),
         waits=[("ld", t_pos)])
    p.ev("scalar", lambda e: e.preload_act_table(AF.Silu), waits=[("ac", C["ac"])], inc=("ac", 1))

    p1s.close()
    p.w("sync", [("vg", C["vg"]), ("ac", C["ac"])])
    ck = contextlib.ExitStack()
    eck = ck.enter_context
    sx = eck(nc.sbuf_tensor("sx", [128, 2, D], F32))
    xlnst = eck(nc.sbuf_tensor("xlnst", [128, 2, D], BF16))
    xfm = eck(nc.sbuf_tensor("xfm", [128, KD, CH], BF16))
    zbuf = eck(nc.sbuf_tensor("zbuf", [128, 3, CH + 3], F32))
    zconv = eck(nc.sbuf_tensor("zconv", [128, 3, CH], F32))
    pwv = eck(nc.sbuf_tensor("pwv", [128, 6, CH], F32))
    qsb = eck(nc.sbuf_tensor("qsb", [128, 4, CH], F32))
    trA = eck(nc.sbuf_tensor("trA", [128, 4, CH], F32))
    trB = eck(nc.sbuf_tensor("trB", [128, 4, CH], F32))
    trC = eck(nc.sbuf_tensor("trC", [128, 4, CH], F32))
    trD = eck(nc.sbuf_tensor("trD", [128, 4, CH], F32))
    trNI = eck(nc.sbuf_tensor("trNI", [128, 4, CH], I32))
    qxre = eck(nc.sbuf_tensor("qxre", [128, 4, CH], F32))
    qxim = eck(nc.sbuf_tensor("qxim", [128, 4, CH], F32))
    orsb = eck(nc.sbuf_tensor("orsb", [128, 2, CH], F32))
    oisb = eck(nc.sbuf_tensor("oisb", [128, 2, CH], F32))
    orb = eck(nc.sbuf_tensor("orb", [128, 2, CH], BF16))
    oib = eck(nc.sbuf_tensor("oib", [128, 2, CH], BF16))
    yact = eck(nc.sbuf_tensor("yact", [128, 6, CH], BF16))
    silu_t = eck(nc.sbuf_tensor("silu_t", [128, CH], F32))
    asb = eck(nc.sbuf_tensor("asb", [128, D], F32))
    sqsc = eck(nc.sbuf_tensor("sqsc", [128, D], F32))
    x1sb = eck(nc.sbuf_tensor("x1sb", [128, D], F32))
    xl3st = eck(nc.sbuf_tensor("xl3st", [128, D], BF16))

    for b in range(B_PER if upto >= 2 else 0):
        vgop(lambda e: e.memset(zbuf[:, :, 0:3], 0.0))
        vgop(lambda e: e.memset(carry_re[:], 0.0))
        vgop(lambda e: e.memset(carry_im[:], 0.0))
        vgop(lambda e: e.memset(carry_den[:], 0.0))
        for c in range(NCH):
            row0 = b * L + c * CH
            for tt_ in range(2):
                r0 = row0 + 128 * tt_
                tl = p.ev("sync", (lambda e, r0=r0, tt_=tt_: e.dma_start(out=sx[:, tt_, :], in_=x_in[r0:r0 + 128, :])),
                          inc=("ld", 16))
                p.ev("vector", (lambda e, tt_=tt_: e.tensor_reduce(stat[:, tt_:tt_ + 1], sx[:, tt_, :], AX.X, ALU.add)),
                     waits=[("ld", tl), ("vg", C["vg"])], inc=("vg", 1))
                p.ev("scalar", (lambda e, tt_=tt_: e.activation(sqsc[:], sx[:, tt_, :], AF.Square, accum_out=stat2[:, tt_:tt_ + 1])),
                     waits=[("ld", tl), ("ac", C["ac"])], inc=("ac", 1))
            tstat_a = C["ac"]
            vgop(lambda e: e.tensor_single_scalar(stat[:, 0:2], stat[:, 0:2], cst[:, 4:5], ALU.mult))
            vgop(lambda e: e.tensor_single_scalar(stat2[:, 0:2], stat2[:, 0:2], cst[:, 4:5], ALU.mult),
                 waits=[("ac", tstat_a)])
            vgop(lambda e: e.tensor_mul(nwt2[:, 0:2], stat[:, 0:2], stat[:, 0:2]))
            vgop(lambda e: e.tensor_tensor(stat2[:, 0:2], stat2[:, 0:2], nwt2[:, 0:2], ALU.subtract))
            newton_rsqrt(2, lambda: stat2[:, 0:2], lambda: rstd_sb[:, 0:2])
            vgop(lambda e: e.tensor_mul(bias_sb[:, 0:2], stat[:, 0:2], rstd_sb[:, 0:2]))
            tln1 = vgop(lambda e: e.tensor_scalar_mul(bias_sb[:, 0:2], bias_sb[:, 0:2], -1.0))
            for tt_ in range(2):
                tap = p.ev("scalar", (lambda e, tt_=tt_: e.activation(xlnst[:, tt_, :], sx[:, tt_, :], AF.Prelu, alpha=1.0,
                                                                      scale=rstd_sb[:, tt_:tt_ + 1], bias=bias_sb[:, tt_:tt_ + 1])),
                           waits=[("vg", tln1), ("gd", C["gd"]), ("ac", C["ac"])], inc=("ac", 1))
                p.ev("gpsimd", (lambda e, tt_=tt_, r0=row0 + 128 * tt_: e.dma_start(out=xln1_d[r0:r0 + 128, :], in_=xlnst[:, tt_, :])),
                     waits=[("ac", tap)], inc=("gd", 16))
            txw = C["gd"]
            for k in range(KD):
                p.ev("sync", (lambda e, k=k, row0=row0: e.dma_start_transpose(xfm[:, k, :], xln1_d[row0:row0 + CH, 128 * k:128 * (k + 1)])),
                     waits=([("gd", txw)] if k == 0 else ()), inc=("ld", 16))
            txf = C["ld"]

            if P2S < 2:
                continue
            mm_w = [("ld", txf), ("vg", C["vg"]), ("ac", C["ac"])]
            for mc, (m0, msz) in enumerate(((0, 128), (128, 128), (256, 32))):
                for k in range(KD):
                    mm(ps[PS_GEN[mc]][0:msz, 0:CH], wmem_b[:, k, m0:m0 + msz], xfm[:, k, :],
                       start=(k == 0), stop=(k == KD - 1),
                       waits=(mm_w if (mc == 0 and k == 0) else ()),
                       inc=(("tn", 1) if k == KD - 1 else None))
            tz = C["tn"]
            for mc, (m0, msz) in enumerate(((0, 128), (128, 128), (256, 32))):
                p.ev("vector", (lambda e, mc=mc, msz=msz: e.tensor_copy(zbuf[0:msz, mc, 3:3 + CH], ps[PS_GEN[mc]][0:msz, 0:CH])),
                     waits=[("tn", tz), ("vg", C["vg"])], inc=("vg", 1))
            tzev = C["vg"]
            for g in range(4):
                for k in range(KD):
                    mm(ps[PS_GEN[g]][:, 0:CH], wq_b[:, k, 128 * g:128 * (g + 1)], xfm[:, k, :],
                       start=(k == 0), stop=(k == KD - 1),
                       waits=([("vg", tzev)] if (k == 0 and g == 0) else ()),
                       inc=(("tn", 1) if k == KD - 1 else None))
            tq = C["tn"]
            for g in range(4):
                p.ev("scalar", (lambda e, g=g: e.activation(qsb[:, g, :], ps[PS_GEN[g]][:, 0:CH], AF.Copy)),
                     waits=([("tn", tq)] if g == 0 else ()), inc=("ac", 1))
            tqev = C["ac"]

            if P2S < 3:
                continue
            first_conv = True
            for gz in range(3):
                msz = 32 if gz == 2 else 128
                wts = [("vg", C["vg"]), ("ac", C["ac"])] if first_conv else [("vg", C["vg"])]
                first_conv = False
                p.ev("vector", (lambda e, gz=gz, msz=msz: e.tensor_single_scalar(
                    zconv[0:msz, gz, :], zbuf[0:msz, gz, 0:CH], convkt_sb[0:msz, gz, 0:1], ALU.mult)),
                    waits=wts, inc=("vg", 1))
                for w_ in range(1, 4):
                    vgop(lambda e, gz=gz, w_=w_, msz=msz: e.scalar_tensor_tensor(
                        zconv[0:msz, gz, :], zbuf[0:msz, gz, w_:w_ + CH], convkt_sb[0:msz, gz, w_:w_ + 1], zconv[0:msz, gz, :], ALU.mult, ALU.add))
            vgop(lambda e: e.tensor_copy(zbuf[:, :, 0:3], zbuf[:, :, CH:CH + 3]))
            tconv = C["vg"]

            if P2S < 4:
                continue
            lw0 = c * CH
            vgop(lambda e: e.tensor_single_scalar(pwv[0:32, 0, :], zconv[0:32, 2, :], score_sb[0:32, 0:1], ALU.mult))
            vgop(lambda e: e.tensor_scalar(pwv[0:32, 0, :], pwv[0:32, 0, :], 20.0, -20.0, ALU.min, ALU.max))
            vgop(lambda e, lw0=lw0: e.tensor_tensor(pwv[0:32, 0, :], pwv[0:32, 0, :], logw_sb[0:32, lw0:lw0 + CH], ALU.add))
            vgop(lambda e: e.tensor_scalar_max(pwv[0:32, 0, :], pwv[0:32, 0, :], -80.0))
            tpw1 = vgop(lambda e: e.tensor_copy(pwv[0:32, 1, :], pwv[0:32, 0, :]))
            p.ev("scalar", (lambda e: e.preload_act_table(AF.Exp)),
                 waits=[("ac", C["ac"])], inc=("ac", 1))
            tth = p.ev("scalar", (lambda e: e.activation(pwv[0:32, 0, :], pwv[0:32, 0, :], AF.Exp)),
                       waits=[("vg", tpw1)], inc=("ac", 1))
            p.ev("scalar", (lambda e: e.preload_act_table(AF.Silu)), inc=("ac", 1))
            vgop(lambda e: e.tensor_copy(pwv[0:32, 4, :], pwv[0:32, 0, :]), waits=[("ac", tth)])
            vgop(lambda e: e.tensor_add(pwv[0:32, 4, 0:1], pwv[0:32, 4, 0:1], carry_den[0:32, 0:1]))
            vgop(lambda e: e.tensor_tensor_scan(pwv[0:32, 5, :], pwv[0:32, 4, :], pwv[0:32, 4, :], 0.0, ALU.add, ALU.bypass))
            vgop(lambda e: e.tensor_copy(carry_den[0:32, 0:1], pwv[0:32, 5, CH - 1:CH]))
            vgop(lambda e: e.tensor_single_scalar(pwv[0:32, 4, :], pwv[0:32, 5, :], cst[0:32, 7:8], ALU.max))
            tpw = vgop(lambda e: e.reciprocal(pwv[0:32, 4, :], pwv[0:32, 4, :]))
            if DBG and b == 0 and c == 0:
                p.ev("gpsimd", (lambda e: e.dma_start(out=dbg["dbgA"], in_=zconv[:])), waits=[("vg", tpw)], inc=("gd", 16))
                p.ev("gpsimd", (lambda e: e.dma_start(out=dbg["dbgB"], in_=pwv[:])), inc=("gd", 16))
                p.ev("gpsimd", (lambda e: e.dma_start(out=dbg["dbgQ"], in_=qsb[:])), inc=("gd", 16))
            for kc in range(2):
                mm(ps[PS_PW][:, CH * kc:CH * (kc + 1)], selpk_sb[:, kc, :], pwv[0:32, 0, :],
                   start=True, stop=True, waits=([("vg", tpw)] if kc == 0 else ()), inc=("tn", 1))
                mm(ps[PS_ID][:, CH * kc:CH * (kc + 1)], selpk_sb[:, kc, :], pwv[0:32, 4, :],
                   start=True, stop=True, inc=("tn", 1))
            tsel = C["tn"]

            if P2S < 5:
                continue
            for kc in range(2):
                for mh in range(2):
                    mbase = 4 * mh
                    for i in range(4):
                        p.ev("scalar", (lambda e, kc=kc, i=i, mbase=mbase: e.activation(
                            trA[:, i, :], zconv[:, kc, :], AF.Prelu, alpha=1.0, scale=theta_sb[:, kc, mbase + i:mbase + i + 1])),
                            waits=([("vg", tconv), ("vg", C["vg"]), ("ac", C["ac"])] if i == 0 else ()), inc=("ac", 1))
                    tphi = C["ac"]
                    vop((lambda e: e.tensor_single_scalar(trNI[:], trA[:], INV2PI, ALU.mult)),
                        waits=[("ac", tphi), ("vg", C["vg"])])
                    vop(lambda e: e.tensor_copy(trB[:], trNI[:]))
                    vop(lambda e: e.scalar_tensor_tensor(trB[:], trB[:], N2PI, trA[:], ALU.mult, ALU.add))
                    vop(lambda e: e.tensor_scalar(trNI[:], trA[:], INV2PI, 0.25, ALU.mult, ALU.add))
                    vop(lambda e: e.tensor_copy(trC[:], trNI[:]))
                    targ = vop(lambda e: e.scalar_tensor_tensor(trC[:], trC[:], N2PI, trA[:], ALU.mult, ALU.add))
                    p.ev("scalar", (lambda e: e.activation(trB[:], trB[:], AF.Sin)), waits=[("vg", targ)], inc=("ac", 1))
                    tsin = p.ev("scalar", (lambda e: e.activation(trC[:], trC[:], AF.Sin, bias=cst[:, 0:1])), inc=("ac", 1))
                    pwb = ps[PS_PW][:, CH * kc:CH * (kc + 1)].unsqueeze(1).to_broadcast([128, 4, CH])
                    vop((lambda e, pwb=pwb: e.tensor_tensor(trA[:], trB[:], pwb, ALU.mult)),
                        waits=[("ac", tsin), ("tn", tsel), ("vg", C["vg"])])
                    vop(lambda e, pwb=pwb: e.tensor_tensor(trC[:], trC[:], pwb, ALU.mult))
                    cb = kc * 8 + mbase
                    vgop(lambda e, cb=cb: e.tensor_add(trC[:, :, 0:1], trC[:, :, 0:1], carry_re[:, cb:cb + 4].unsqueeze(2)))
                    vgop(lambda e, cb=cb: e.tensor_add(trA[:, :, 0:1], trA[:, :, 0:1], carry_im[:, cb:cb + 4].unsqueeze(2)))
                    for i in range(4):
                        vgop(lambda e, i=i: e.tensor_tensor_scan(trD[:, i, :], trC[:, i, :], trC[:, i, :], 0.0, ALU.add, ALU.bypass))
                    for i in range(4):
                        vgop(lambda e, i=i: e.tensor_tensor_scan(trB[:, i, :], trA[:, i, :], trA[:, i, :], 0.0, ALU.add, ALU.bypass))
                    vgop(lambda e, cb=cb: e.tensor_copy(carry_re[:, cb:cb + 4].unsqueeze(2), trD[:, :, CH - 1:CH]))
                    tsc2 = vgop(lambda e, cb=cb: e.tensor_copy(carry_im[:, cb:cb + 4].unsqueeze(2), trB[:, :, CH - 1:CH]))
                    if DBG and b == 0 and c == 0 and kc == 0 and mh == 0:
                        p.ev("gpsimd", (lambda e: e.dma_start(out=dbg["dbgC"], in_=trD[:])), waits=[("vg", tsc2)], inc=("gd", 16))
                        p.ev("gpsimd", (lambda e: e.dma_start(out=dbg["dbgC2"], in_=trB[:])), inc=("gd", 16))
                    for ri, qx in ((0, qxre), (1, qxim)):
                        for i in range(4):
                            m_ = mbase + i
                            colblk = (m_ * 2 + ri) * 2
                            bank = PS_QX[i // 2]
                            off = CH * (i % 2)
                            mm(ps[bank][:, off:off + CH], selq_sb[:, 128 * colblk:128 * (colblk + 1)], qsb[:, 2 * kc, :],
                               start=True, stop=False,
                               waits=([("ac", tqev), ("ac", C["ac"])] if i == 0 else ()))
                            mm(ps[bank][:, off:off + CH], selq_sb[:, 128 * (colblk + 1):128 * (colblk + 2)], qsb[:, 2 * kc + 1, :],
                               start=False, stop=True, inc=("tn", 1))
                        tqx = C["tn"]
                        p.ev("scalar", (lambda e, qx=qx: e.activation(qx[:, 0:2, :].rearrange("q a v -> q (a v)"), ps[PS_QX[0]][:, 0:2 * CH], AF.Copy)),
                             waits=[("tn", tqx)], inc=("ac", 1))
                        p.ev("scalar", (lambda e, qx=qx: e.activation(qx[:, 2:4, :].rearrange("q a v -> q (a v)"), ps[PS_QX[1]][:, 0:2 * CH], AF.Copy)),
                             inc=("ac", 1))
                    tqxe = C["ac"]
                    if DBG and b == 0 and c == 0 and kc == 0 and mh == 0:
                        p.ev("gpsimd", (lambda e: e.dma_start(out=dbg["dbgQX"], in_=qxre[:])), waits=[("ac", tqxe)], inc=("gd", 16))
                    vop((lambda e: e.tensor_mul(trA[:], trD[:], qxre[:])),
                        waits=[("ac", tqxe), ("vg", C["vg"])])
                    vop(lambda e: e.tensor_mul(trC[:], trB[:], qxim[:]))
                    vop(lambda e: e.tensor_add(trA[:], trA[:], trC[:]))
                    vop(lambda e: e.tensor_mul(trC[:], trB[:], qxre[:]))
                    vop(lambda e: e.tensor_mul(trB[:], trD[:], qxim[:]))
                    vop(lambda e: e.tensor_tensor(trC[:], trC[:], trB[:], ALU.subtract))
                    vA = bass.AP(trA, 0, [[4 * CH, 128], [1, CH], [CH, 4]])
                    vC = bass.AP(trC, 0, [[4 * CH, 128], [1, CH], [CH, 4]])
                    if mh == 0:
                        vgop(lambda e, vA=vA, kc=kc: e.tensor_reduce(orsb[:, kc, :], vA, AX.X, ALU.add))
                        vgop(lambda e, vC=vC, kc=kc: e.tensor_reduce(oisb[:, kc, :], vC, AX.X, ALU.add))
                    else:
                        vgop(lambda e, vA=vA: e.tensor_reduce(trD[:, 0, :], vA, AX.X, ALU.add))
                        vgop(lambda e, vC=vC: e.tensor_reduce(trD[:, 1, :], vC, AX.X, ALU.add))
                        vgop(lambda e, kc=kc: e.tensor_add(orsb[:, kc, :], orsb[:, kc, :], trD[:, 0, :]))
                        vgop(lambda e, kc=kc: e.tensor_add(oisb[:, kc, :], oisb[:, kc, :], trD[:, 1, :]))
                idb = ps[PS_ID][:, CH * kc:CH * (kc + 1)]
                vgop(lambda e, kc=kc, idb=idb: e.tensor_tensor(orsb[:, kc, :], orsb[:, kc, :], idb, ALU.mult))
                vgop(lambda e, kc=kc, idb=idb: e.tensor_tensor(oisb[:, kc, :], oisb[:, kc, :], idb, ALU.mult))
                vgop(lambda e, kc=kc: e.tensor_copy(orb[:, kc, :], orsb[:, kc, :]))
                vgop(lambda e, kc=kc: e.tensor_copy(oib[:, kc, :], oisb[:, kc, :]))
            toi = C["vg"]
            if DBG and b == 0 and c == 0:
                p.ev("gpsimd", (lambda e: e.dma_start(out=dbg["dbgD"], in_=orsb[:])), waits=[("vg", toi)], inc=("gd", 16))

            if P2S < 6:
                continue
            ytok = {}
            for j in range(6):
                bv = PS_GEN[(2 * j) % 4]
                bg = PS_GEN[(2 * j + 1) % 4]
                yw = [("vg", toi), ("ac", C["ac"])] if j == 0 else []
                if j >= 2:
                    yw = yw + [("vg", ytok[j - 2][0]), ("ac", ytok[j - 2][1])]
                mm(ps[bv][:, 0:CH], rebd_b[:, 0, 128 * j:128 * (j + 1)], orb[:, 0, :], start=True, stop=False, waits=yw)
                mm(ps[bv][:, 0:CH], imbd_b[:, 0, 128 * j:128 * (j + 1)], oib[:, 0, :], start=False, stop=False)
                mm(ps[bv][:, 0:CH], rebd_b[:, 1, 128 * j:128 * (j + 1)], orb[:, 1, :], start=False, stop=False)
                mm(ps[bv][:, 0:CH], imbd_b[:, 1, 128 * j:128 * (j + 1)], oib[:, 1, :], start=False, stop=True, inc=("tn", 1))
                mm(ps[bg][:, 0:CH], rebd_b[:, 0, 768 + 128 * j:768 + 128 * (j + 1)], orb[:, 0, :], start=True, stop=False)
                mm(ps[bg][:, 0:CH], imbd_b[:, 0, 768 + 128 * j:768 + 128 * (j + 1)], oib[:, 0, :], start=False, stop=False)
                mm(ps[bg][:, 0:CH], rebd_b[:, 1, 768 + 128 * j:768 + 128 * (j + 1)], orb[:, 1, :], start=False, stop=False)
                ty = mm(ps[bg][:, 0:CH], imbd_b[:, 1, 768 + 128 * j:768 + 128 * (j + 1)], oib[:, 1, :], start=False, stop=True, inc=("tn", 1))
                tsi = p.ev("scalar", (lambda e, bg=bg: e.activation(silu_t[:], ps[bg][:, 0:CH], AF.Silu)),
                           waits=[("tn", ty), ("ac", C["ac"])], inc=("ac", 1))
                tyj = p.ev("vector", (lambda e, j=j, bv=bv: e.tensor_tensor(yact[:, j, :], ps[bv][:, 0:CH], silu_t[:], ALU.mult)),
                     waits=[("ac", tsi), ("vg", C["vg"])], inc=("vg", 1))
                ytok[j] = (tyj, tsi)
            tya = C["vg"]
            if DBG and b == 0 and c == 0:
                p.ev("gpsimd", (lambda e: e.dma_start(out=dbg["dbgE"], in_=yact[:])), waits=[("vg", tya)], inc=("gd", 16))

            if P2S < 7:
                continue
            for tt_ in range(2):
                r0 = row0 + 128 * tt_
                for dc in range(4):
                    for j in range(6):
                        mm(ps[PS_GEN[dc]][:, 0:512], yact[:, j, 128 * tt_:128 * (tt_ + 1)], wout_b[:, j, 512 * dc:512 * (dc + 1)],
                           start=(j == 0), stop=(j == 5),
                           waits=([("vg", tya), ("ac", C["ac"])] if (dc == 0 and j == 0) else ()),
                           inc=(("tn", 1) if j == 5 else None))
                tao = C["tn"]
                for dc in range(4):
                    p.ev("scalar", (lambda e, dc=dc: e.activation(asb[:, 512 * dc:512 * (dc + 1)], ps[PS_GEN[dc]][:, 0:512],
                                                                  AF.Copy, accum_out=stat[:, 8 + dc:9 + dc])),
                         waits=([("tn", tao), ("ac", C["ac"]), ("vg", C["vg"])] if dc == 0 else ()), inc=("ac", 1))
                tev = p.ev("scalar", (lambda e: e.activation(sqsc[:], asb[:], AF.Square, accum_out=stat2[:, 8:9])),
                           inc=("ac", 1))
                if DBG and b == 0 and c == 0 and tt_ == 0:
                    p.ev("gpsimd", (lambda e: e.dma_start(out=dbg["dbgF"], in_=asb[:])), waits=[("ac", tev)], inc=("gd", 16))
                vgop(lambda e: e.tensor_reduce(stat[:, 12:13], stat[:, 8:12], AX.X, ALU.add), waits=[("ac", tev)])
                vgop(lambda e: e.tensor_single_scalar(stat[:, 12:13], stat[:, 12:13], cst[:, 4:5], ALU.mult))
                vgop(lambda e: e.tensor_single_scalar(stat2[:, 8:9], stat2[:, 8:9], cst[:, 4:5], ALU.mult))
                vgop(lambda e: e.tensor_mul(nwt2[:, 0:1], stat[:, 12:13], stat[:, 12:13]))
                vgop(lambda e: e.tensor_tensor(stat2[:, 8:9], stat2[:, 8:9], nwt2[:, 0:1], ALU.subtract))
                newton_rsqrt(1, lambda: stat2[:, 8:9], lambda: rstd_sb[:, 4:5])
                tln2 = vgop(lambda e: e.tensor_mul(bias_sb[:, 4:5], stat[:, 12:13], rstd_sb[:, 4:5]))
                vop((lambda e, tt_=tt_: e.scalar_tensor_tensor(x1sb[:], asb[:], rstd_sb[:, 4:5], sx[:, tt_, :], ALU.mult, ALU.add)),
                    waits=[("vg", tln2), ("gd", C["gd"])])
                tx1 = vop(lambda e: e.tensor_single_scalar(x1sb[:], x1sb[:], bias_sb[:, 4:5], ALU.subtract))
                p.ev("gpsimd", (lambda e, r0=r0: e.dma_start(out=x1_d[r0:r0 + 128, :], in_=x1sb[:])),
                     waits=[("vg", tx1)], inc=("gd", 16))
                vgop(lambda e: e.tensor_reduce(stat[:, 13:14], x1sb[:], AX.X, ALU.add))
                tq3 = p.ev("scalar", (lambda e: e.activation(sqsc[:], x1sb[:], AF.Square, accum_out=stat2[:, 9:10])),
                           waits=[("vg", C["vg"]), ("ac", C["ac"])], inc=("ac", 1))
                vgop(lambda e: e.tensor_single_scalar(stat[:, 13:14], stat[:, 13:14], cst[:, 4:5], ALU.mult))
                vgop(lambda e: e.tensor_single_scalar(stat2[:, 9:10], stat2[:, 9:10], cst[:, 4:5], ALU.mult),
                     waits=[("ac", tq3)])
                vgop(lambda e: e.tensor_mul(nwt2[:, 0:1], stat[:, 13:14], stat[:, 13:14]))
                vgop(lambda e: e.tensor_tensor(stat2[:, 9:10], stat2[:, 9:10], nwt2[:, 0:1], ALU.subtract))
                newton_rsqrt(1, lambda: stat2[:, 9:10], lambda: rstd_sb[:, 5:6])
                vgop(lambda e: e.tensor_mul(bias_sb[:, 5:6], stat[:, 13:14], rstd_sb[:, 5:6]))
                tln3 = vgop(lambda e: e.tensor_scalar_mul(bias_sb[:, 5:6], bias_sb[:, 5:6], -1.0))
                tl3 = p.ev("scalar", (lambda e: e.activation(xl3st[:], x1sb[:], AF.Prelu, alpha=1.0, scale=rstd_sb[:, 5:6], bias=bias_sb[:, 5:6])),
                           waits=[("vg", tln3), ("gd", C["gd"])], inc=("ac", 1))
                p.ev("gpsimd", (lambda e, r0=r0: e.dma_start(out=xln3_d[r0:r0 + 128, :], in_=xl3st[:])),
                     waits=[("ac", tl3)], inc=("gd", 16))
            p.w("sync", [("vg", C["vg"]), ("ac", C["ac"]), ("tn", C["tn"])])
    t_attn_done = [("vg", C["vg"]), ("ac", C["ac"]), ("tn", C["tn"]), ("gd", C["gd"])]

    ck.close()
    p2.close()

    dqs = contextlib.ExitStack()
    ecd = dqs.enter_context
    dq_pk = ecd(nc.sbuf_tensor("dq_pk", [128, 2, 1024], I32))
    dq_sc = ecd(nc.sbuf_tensor("dq_sc", [128, 2, 64], F32))
    dq_bf = ecd(nc.sbuf_tensor("dq_bf", [128, 1024], F32))
    dq_hi = ecd(nc.sbuf_tensor("dq_hi", [128, 1024], I32))
    dq_hf = ecd(nc.sbuf_tensor("dq_hf", [128, 1024], F32))
    dq_c = ecd(nc.sbuf_tensor("dq_c", [128, 2048], BF16))
    dq_d1 = ecd(nc.sbuf_tensor("dq_d1", [128, 2048], BF16))
    dq_d2 = ecd(nc.sbuf_tensor("dq_d2", [128, 2048], BF16))
    dq_d3 = ecd(nc.sbuf_tensor("dq_d3", [128, 2048], BF16))
    dq_d4 = ecd(nc.sbuf_tensor("dq_d4", [128, 2048], BF16))
    dq_e = ecd(nc.sbuf_tensor("dq_e", [128, 2048], BF16))
    dq_sx = ecd(nc.sbuf_tensor("dq_sx", [128, 2048], F32))
    dq_o = ecd(nc.sbuf_tensor("dq_o", [128, 2, 2048], BF16))

    dq_free = {}
    dq_freeg = {}
    dq_store = {}
    dq_w11 = {}
    dq_w14 = {}
    dq_state = {"r": 0, "first": True}

    GQ, GS = (("gpsimd", "gp") if int(os.environ.get("K_GP", "0")) else ("vector", "vg"))

    def gop(fn, waits=()):
        return p.ev(GQ, fn, waits=[(GS, C[GS])] + list(waits), inc=(GS, 1))

    def emit_dec_tile(pk_in, sc_in, wdst, rr, cc):
        r = dq_state["r"]
        dq_state["r"] = r + 1
        bi = r % 2
        lw = [("vg", dq_free.get(r - 2, 0)), (GS, dq_freeg.get(r - 2, 0))]
        if dq_state["first"]:
            lw = lw + list(t_attn_done)
            dq_state["first"] = False
        tl = p.ev("scalar", (lambda e, rr=rr, cc=cc, bi=bi, pk_in=pk_in:
                             e.dma_start(out=dq_pk[:, bi, :], in_=pk_in[128 * rr:128 * (rr + 1), 1024 * cc:1024 * (cc + 1)])),
                  waits=lw, inc=("ld3", 16))
        tl2 = p.ev("scalar", (lambda e, rr=rr, cc=cc, bi=bi, sc_in=sc_in:
                              e.dma_start(out=dq_sc[:, bi, :], in_=sc_in[128 * rr:128 * (rr + 1), 64 * cc:64 * (cc + 1)])),
                   inc=("ld3", 16))
        scx = dq_sx[:].rearrange("q (v w) -> q v w", w=32)
        gop(lambda e, bi=bi, scx=scx: e.tensor_copy(scx[:, :, 0:1], dq_sc[:, bi, :].unsqueeze(2)),
            waits=[("ld3", tl2)])
        for w_ in (1, 2, 4, 8, 16):
            gop(lambda e, w_=w_, scx=scx: e.tensor_copy(scx[:, :, w_:2 * w_], scx[:, :, 0:w_]))
        dq_freeg[r] = C[GS]
        vop((lambda e, bi=bi: e.tensor_copy(dq_bf[:], dq_pk[:, bi, :])),
            waits=[("ld3", tl), ("vg", C["vg"])])
        vop(lambda e: e.tensor_scalar(dq_hi[:], dq_bf[:], 1.0 / 16.0, -0.46875, ALU.mult, ALU.add))
        vop(lambda e: e.tensor_copy(dq_hf[:], dq_hi[:]))
        vop(lambda e: e.scalar_tensor_tensor(dq_c[:, 0:2048:2], dq_hf[:], -16.0, dq_bf[:], ALU.mult, ALU.add))
        vop(lambda e: e.tensor_copy(dq_c[:, 1:2048:2], dq_hf[:]))
        v6 = vop((lambda e: e.tensor_single_scalar(dq_d1[:], dq_c[:], 8.0, ALU.is_ge)),
                 waits=[(GS, dq_w11.get(r - 1, 0))])
        tm_ = vop((lambda e: e.scalar_tensor_tensor(dq_d2[:], dq_d1[:], -8.0, dq_c[:], ALU.mult, ALU.add)),
                  waits=[(GS, dq_w14.get(r - 1, 0))])
        dq_w11[r] = gop(lambda e: e.tensor_scalar(dq_e[:], dq_d1[:], -2.0, 1.0, ALU.mult, ALU.add),
                        waits=[("vg", v6)])
        p.ev("scalar", (lambda e: e.activation(dq_d3[:], dq_d2[:], AF.Relu, bias=cst[:, 2:3])),
             waits=[("vg", tm_), ("ac", C["ac"])], inc=("ac", 1))
        ta_ = p.ev("scalar", (lambda e: e.activation(dq_d4[:], dq_d2[:], AF.Relu, bias=cst[:, 3:4])),
                   inc=("ac", 1))
        vop((lambda e: e.tensor_add(dq_d2[:], dq_d2[:], dq_d3[:])), waits=[("ac", ta_)])
        v10 = vop(lambda e: e.scalar_tensor_tensor(dq_d2[:], dq_d2[:], 0.5, dq_d4[:], ALU.mult, ALU.add))
        dq_free[r] = v10
        dq_w14[r] = gop(lambda e: e.tensor_mul(dq_e[:], dq_e[:], dq_d2[:]), waits=[("vg", v10)])
        t14 = gop((lambda e, bi=bi: e.tensor_tensor(dq_o[:, bi, :], dq_e[:], dq_sx[:], ALU.mult)),
                  waits=[("gd", dq_store.get(r - 2, 0))])
        dq_store[r] = p.ev("gpsimd", (lambda e, rr=rr, cc=cc, bi=bi, wdst=wdst:
                                      e.dma_start(out=wdst[128 * rr:128 * (rr + 1), 2048 * cc:2048 * (cc + 1)], in_=dq_o[:, bi, :])),
                           waits=[(GS, t14)], inc=("gd", 16))

    p3 = contextlib.ExitStack()
    ec3 = p3.enter_context
    TH = T // 2
    xfm3 = ec3(nc.sbuf_tensor("xfm3", [128, KD, TH], BF16))
    wgt = [ec3(nc.sbuf_tensor(f"wgt{i}", [128, KD, 256], BF16)) for i in range(2)]
    wut = [ec3(nc.sbuf_tensor(f"wut{i}", [128, KD, 256], BF16)) for i in range(2)]
    gl1 = ec3(nc.sbuf_tensor("gl1", [128, 512], F32))
    gl2 = ec3(nc.sbuf_tensor("gl2", [128, 512], F32))
    gl3 = ec3(nc.sbuf_tensor("gl3", [128, 512], F32))
    ibuf = [ec3(nc.sbuf_tensor(f"ibuf{i}", [128, 512], BF16)) for i in range(2)]

    NG = (NMLP // 256) if upto >= 3 else 0
    dec_gu_gd = {}
    dn_gd = {}
    dn_tiles = [(rr, cc) for cc in range(NMLP // 2048) for rr in range(D // 128)]
    mm_tok = {}
    ev_tok = {}
    st_tok = {}

    def emit_dec_group(g):
        for pk_in, sc_in, wdst in ((gpk_in, gsc_in, wg_d), (upk_in, usc_in, wu_d)):
            for j in (0, 1):
                emit_dec_tile(pk_in, sc_in, wdst, 2 * g + j, 0)
        dec_gu_gd[g] = C["gd"]

    for g0 in range(min(4, NG)):
        emit_dec_group(g0)
    for half in range(2 if upto >= 3 else 0):
        t0h = TH * half
        xw = list(t_attn_done) if half == 0 else [("tn", C["tn"])]
        for k in range(KD):
            for t4 in range(4):
                p.ev("sync", (lambda e, k=k, t4=t4, t0h=t0h: e.dma_start_transpose(
                    xfm3[:, k, 512 * t4:512 * (t4 + 1)], xln3_d[t0h + 512 * t4:t0h + 512 * (t4 + 1), 128 * k:128 * (k + 1)])),
                    waits=(xw if (k == 0 and t4 == 0) else ()), inc=("ld2", 16))
        txf3 = C["ld2"]
        for ng in range(NG):
            gi = half * NG + ng
            bi = gi % 2
            wl_waits = [("ld2", txf3)]
            if half == 0:
                wl_waits.append(("gd", dec_gu_gd[ng]))
            if gi >= 2:
                wl_waits.append(("tn", mm_tok[gi - 2]))
            for k in range(KD):
                p.ev("sync", (lambda e, ng=ng, bi=bi, k=k: e.dma_start_transpose(
                    wgt[bi][:, k, :], wg_d[256 * ng:256 * (ng + 1), 128 * k:128 * (k + 1)])),
                    waits=(wl_waits if k == 0 else ()), inc=("ld2", 16))
            for k in range(KD):
                p.ev("sync", (lambda e, ng=ng, bi=bi, k=k: e.dma_start_transpose(
                    wut[bi][:, k, :], wu_d[256 * ng:256 * (ng + 1), 128 * k:128 * (k + 1)])),
                    inc=("ld2", 16))
            twl = C["ld2"]
            for ns in range(2):
                for t4 in range(4):
                    it = (gi * 2 + ns) * 4 + t4
                    bgk = it % 4
                    buk = 4 + it % 4
                    mmw = []
                    if ns == 0 and t4 == 0:
                        mmw.append(("ld2", twl))
                    if it >= 4:
                        mmw.append(("vg", ev_tok.get(it - 4, 0)))
                    for k in range(KD):
                        mm(ps[bgk][:, 0:512], wgt[bi][:, k, 128 * ns:128 * ns + 128], xfm3[:, k, 512 * t4:512 * (t4 + 1)],
                           start=(k == 0), stop=(k == KD - 1), waits=(mmw if k == 0 else ()))
                    for k in range(KD):
                        mm(ps[buk][:, 0:512], wut[bi][:, k, 128 * ns:128 * ns + 128], xfm3[:, k, 512 * t4:512 * (t4 + 1)],
                           start=(k == 0), stop=(k == KD - 1), inc=(("tn", 1) if k == KD - 1 else None))
                    tmm = C["tn"]
                    if ns == 1 and t4 == 3:
                        mm_tok[gi] = tmm
                    tg1 = p.ev("scalar", (lambda e, bgk=bgk: e.activation(gl1[:], ps[bgk][:, 0:512], AF.Square)),
                               waits=[("tn", tmm), ("ac", C["ac"]), ("vg", C["vg"])], inc=("ac", 1))
                    vop((lambda e: e.tensor_scalar(gl2[:], gl1[:], GELU_C3, GELU_C1, ALU.mult, ALU.add)),
                        waits=[("ac", tg1), ("vg", C["vg"])])
                    tg2 = vop(lambda e, bgk=bgk: e.tensor_tensor(gl2[:], gl2[:], ps[bgk][:, 0:512], ALU.mult))
                    tg3 = p.ev("scalar", (lambda e: e.activation(gl3[:], gl2[:], AF.Tanh)),
                               waits=[("vg", tg2)], inc=("ac", 1))
                    vop((lambda e: e.tensor_scalar(gl3[:], gl3[:], 0.5, 0.5, ALU.mult, ALU.add)),
                        waits=[("ac", tg3), ("vg", C["vg"])])
                    vop(lambda e, bgk=bgk: e.tensor_tensor(gl3[:], gl3[:], ps[bgk][:, 0:512], ALU.mult))
                    ibb = ibuf[it % 2]
                    tev3 = vop((lambda e, buk=buk, ibb=ibb: e.tensor_tensor(ibb[:], gl3[:], ps[buk][:, 0:512], ALU.mult)),
                               waits=[("gd2", st_tok.get(it - 2, 0))])
                    ev_tok[it] = tev3
                    n0 = 256 * ng + 128 * ns
                    t0 = t0h + 512 * t4
                    st_tok[it] = p.ev("gpsimd", (lambda e, n0=n0, t0=t0, ibb=ibb: e.dma_start(
                        out=inter_d[n0:n0 + 128, t0:t0 + 512], in_=ibb[:])),
                        waits=[("vg", tev3)], inc=("gd2", 16))
            if half == 0 and ng + 4 < NG:
                emit_dec_group(ng + 4)
            if half == 1 and 2 * ng < len(dn_tiles):
                for j in (0, 1):
                    rr, cc = dn_tiles[2 * ng + j]
                    emit_dec_tile(dpk_in, dsc_in, wd_d, rr, cc)
                    dn_gd[2 * ng + j] = C["gd"]
    t_p3 = [("vg", C["vg"]), ("ac", C["ac"]), ("tn", C["tn"]), ("gd2", C["gd2"]), ("gd", C["gd"]), ("gp", C["gp"])]

    p3.close()
    dqs.close()
    p4 = contextlib.ExitStack()
    ec4 = p4.enter_context
    isb = ec4(nc.sbuf_tensor("isb", [128, KD, T], BF16))
    wdt = [ec4(nc.sbuf_tensor(f"wdt{i}", [128, KD, 512], BF16)) for i in range(2)]
    dsb = [ec4(nc.sbuf_tensor(f"dsb{i}", [128, 512], F32)) for i in range(2)]

    d_ev = {}
    d_st = {}
    acc_prev = 0
    for nm in range(min(P4NM, 8) if upto >= 4 else 0):
        il_w = list(t_p3) if nm == 0 else [("tn", C["tn"])]
        for k in range(KD):
            p.ev("sync", (lambda e, nm=nm, k=k: e.dma_start(
                out=isb[:, k, :], in_=inter_d[2048 * nm + 128 * k:2048 * nm + 128 * (k + 1), :])),
                waits=(il_w if k == 0 else ()), inc=("ld2", 16))
        til = C["ld2"]
        for hb in range(4):
            bi = hb % 2
            w4 = [("gd", dn_gd.get(nm * 16 + 4 * hb + 3, 0))]
            if nm > 0 or hb >= 2:
                w4.append(("tn", C["tn"]))
            for k in range(KD):
                p.ev("sync", (lambda e, nm=nm, hb=hb, bi=bi, k=k: e.dma_start_transpose(
                    wdt[bi][:, k, :], wd_d[512 * hb:512 * (hb + 1), 2048 * nm + 128 * k:2048 * nm + 128 * (k + 1)])),
                    waits=(w4 if k == 0 else ()), inc=("ld2", 16))
            twd = C["ld2"]
            for t32 in range(32):
                git = nm * 128 + hb * 32 + t32
                bank = PS_GEN[git % 4]
                mmw = []
                if t32 == 0:
                    mmw += [("ld2", til), ("ld2", twd)]
                if git >= 4:
                    mmw.append(("vg", d_ev.get(git - 4, 0)))
                for k in range(KD):
                    mm(ps[bank][:, 0:512], isb[:, k, 128 * t32:128 * (t32 + 1)], wdt[bi][:, k, :],
                       start=(k == 0), stop=(k == KD - 1), waits=(mmw if k == 0 else ()),
                       inc=(("tn", 1) if k == KD - 1 else None))
                tmm = C["tn"]
                db = dsb[git % 2]
                tev4 = p.ev("vector", (lambda e, bank=bank, db=db: e.tensor_copy(db[:], ps[bank][:, 0:512])),
                            waits=[("tn", tmm), ("vg", C["vg"]), ("gd2", d_st.get(git - 2, 0))], inc=("vg", 1))
                d_ev[git] = tev4
                accw = [("vg", tev4)]
                if nm > 0:
                    accw.append(("gd2", acc_prev))
                if nm == 0:
                    d_st[git] = p.ev("gpsimd", (lambda e, hb=hb, t32=t32, db=db: e.dma_start(
                        out=out2_d[128 * t32:128 * (t32 + 1), 512 * hb:512 * (hb + 1)], in_=db[:])),
                        waits=accw, inc=("gd2", 16))
                else:
                    d_st[git] = p.ev("gpsimd", (lambda e, hb=hb, t32=t32, db=db: e.dma_start(
                        out=out2_d[128 * t32:128 * (t32 + 1), 512 * hb:512 * (hb + 1)], in_=db[:],
                        accum_op=ALU.add)),
                        waits=accw, inc=("gd2", 16))
        acc_prev = C["gd2"]
    t_p4_gd = C["gd2"]

    p4.close()
    p5 = contextlib.ExitStack()
    ec5 = p5.enter_context
    o2sb = ec5(nc.sbuf_tensor("o2sb", [128, D], F32))
    x1rb = ec5(nc.sbuf_tensor("x1rb", [128, D], F32))
    sq5 = ec5(nc.sbuf_tensor("sq5", [128, D], F32))
    outsb = ec5(nc.sbuf_tensor("outsb", [128, D], F32))

    for t32 in range(32 if upto >= 5 else 0):
        r0 = 128 * t32
        lw5 = [("vg", C["vg"]), ("ac", C["ac"])]
        if t32 == 0:
            lw5.append(("gd2", t_p4_gd))
        tl = p.ev("sync", (lambda e, r0=r0: e.dma_start(out=o2sb[:], in_=out2_d[r0:r0 + 128, :])),
                  waits=lw5, inc=("ld2", 16))
        tl2 = p.ev("sync", (lambda e, r0=r0: e.dma_start(out=x1rb[:], in_=x1_d[r0:r0 + 128, :])), inc=("ld2", 16))
        p.ev("vector", (lambda e: e.tensor_reduce(stat[:, 14:15], o2sb[:], AX.X, ALU.add)),
             waits=[("ld2", tl), ("vg", C["vg"])], inc=("vg", 1))
        tq5 = p.ev("scalar", (lambda e: e.activation(sq5[:], o2sb[:], AF.Square, accum_out=stat2[:, 10:11])),
                   waits=[("ld2", tl), ("ac", C["ac"])], inc=("ac", 1))
        vgop(lambda e: e.tensor_single_scalar(stat[:, 14:15], stat[:, 14:15], cst[:, 4:5], ALU.mult))
        vgop(lambda e: e.tensor_single_scalar(stat2[:, 10:11], stat2[:, 10:11], cst[:, 4:5], ALU.mult),
             waits=[("ac", tq5)])
        vgop(lambda e: e.tensor_mul(nwt2[:, 0:1], stat[:, 14:15], stat[:, 14:15]))
        vgop(lambda e: e.tensor_tensor(stat2[:, 10:11], stat2[:, 10:11], nwt2[:, 0:1], ALU.subtract))
        newton_rsqrt(1, lambda: stat2[:, 10:11], lambda: rstd_sb[:, 6:7])
        tln4 = vgop(lambda e: e.tensor_mul(bias_sb[:, 6:7], stat[:, 14:15], rstd_sb[:, 6:7]))
        vop((lambda e: e.scalar_tensor_tensor(outsb[:], o2sb[:], rstd_sb[:, 6:7], x1rb[:], ALU.mult, ALU.add)),
            waits=[("vg", tln4), ("ld2", tl2), ("gd2", C["gd2"])])
        tout = vop(lambda e: e.tensor_single_scalar(outsb[:], outsb[:], bias_sb[:, 6:7], ALU.subtract))
        p.ev("gpsimd", (lambda e, r0=r0: e.dma_start(out=out_d[r0:r0 + 128, :], in_=outsb[:])),
             waits=[("vg", tout)], inc=("gd2", 16))
    if DBG:
        p.ev("gpsimd", (lambda e: e.dma_start(out=dbg["dbgG"], in_=xln1_d[0:256, :])), waits=[("gd2", C["gd2"]), ("gd", C["gd"]), ("vg", C["vg"])], inc=("gd2", 16))
        p.ev("gpsimd", (lambda e: e.dma_start(out=dbg["dbgH"], in_=x1_d[0:256, :])), inc=("gd2", 16))
        p.ev("gpsimd", (lambda e: e.dma_start(out=dbg["dbgI"], in_=wg_d[0:128, :])), inc=("gd2", 16))
        p.ev("gpsimd", (lambda e: e.dma_start(out=dbg["dbgJ"], in_=inter_d[0:128, :])), inc=("gd2", 16))
        p.ev("gpsimd", (lambda e: e.dma_start(out=dbg["dbgK"], in_=out2_d[0:256, :])), inc=("gd2", 16))
    p.w("sync", [("gd2", C["gd2"])])
    p.ev("sync", lambda e: e.nop())

    sems = {}
    with contextlib.ExitStack() as semstack:
        for s in sorted(p.sem_names):
            sems[s] = semstack.enter_context(nc.semaphore(s))
        with nc.Block() as block:
            @block.sync
            def _(eng):
                p.replay("sync", eng, sems)

            @block.tensor
            def _(eng):
                p.replay("tensor", eng, sems)

            @block.vector
            def _(eng):
                p.replay("vector", eng, sems)

            @block.scalar
            def _(eng):
                p.replay("scalar", eng, sems)

            @block.gpsimd
            def _(eng):
                p.replay("gpsimd", eng, sems)
    p5.close()
    stack.close()
    return nc


def _host_layout(inputs):
    att_x = np.ascontiguousarray(inputs["L0_attn_x"])
    W_mem = np.ascontiguousarray(inputs["L0_attn_W_mem"], dtype=np.float32)
    conv_k = np.asarray(inputs["L0_attn_conv_k"], dtype=np.float32)
    W_q = np.ascontiguousarray(inputs["L0_attn_W_q"], dtype=np.float32)
    theta = np.asarray(inputs["L0_attn_theta"], dtype=np.float32).reshape(32, 8, 8)
    dec = np.asarray(inputs["L0_attn_decay_slopes"], dtype=np.float32)
    anc = np.asarray(inputs["L0_attn_anchor_slopes"], dtype=np.float32)
    score = np.asarray(inputs["L0_attn_score_scale"], dtype=np.float32)
    W_re = np.asarray(inputs["L0_attn_W_re"], dtype=np.float32)
    W_im = np.asarray(inputs["L0_attn_W_im"], dtype=np.float32)
    nsc = np.asarray(inputs["L0_attn_norm_scale"], dtype=np.float32).reshape(256, 1)
    W_out = np.ascontiguousarray(inputs["L0_attn_W_out"], dtype=np.float32)

    convkt = np.zeros((384, 4), np.float32)
    convkt[0:288, :] = conv_k[:, 0, :].T
    theta_t = np.ascontiguousarray(theta.reshape(256, 8))
    slopes = np.concatenate([dec, anc]).reshape(32, 1).astype(np.float32)
    score_c = score.reshape(32, 1).astype(np.float32)

    pos = np.arange(L, dtype=np.float32)
    posdist = np.empty((32, L), np.float32)
    posdist[0:28, :] = (L - 1) - pos[None, :]
    posdist[28:32, :] = pos[None, :]

    def blockdiag(W):
        out = np.zeros((256, 1536), np.float32)
        for k in range(32):
            for h in range(8):
                out[k * 8 + h, 24 * k:24 * (k + 1)] = W[k, h, 0:24]
                out[k * 8 + h, 768 + 24 * k:768 + 24 * (k + 1)] = W[k, h, 24:48]
        return out
    rebd = blockdiag(W_re)
    imbd = blockdiag(W_im)

    selpk = np.zeros((32, 2, 128), np.float32)
    for cc in range(2):
        for i in range(128):
            selpk[16 * cc + i // 8, cc, i] = 1.0
    selq = np.zeros((128, 8, 2, 2, 128), np.float32)
    for m in range(8):
        for ri in range(2):
            for g in range(2):
                for i in range(128):
                    if (i // 64) == g:
                        selq[(i % 8) * 16 + 2 * m + ri, m, ri, g, i] = 1.0
    selq = np.ascontiguousarray(selq.reshape(128, 4096))

    common = {
        "wmem": W_mem, "convkt": convkt, "wq": W_q, "theta": theta_t,
        "slopes": slopes, "score": score_c, "rebd": rebd, "imbd": imbd,
        "nsc": nsc, "wout": W_out, "posdist": posdist,
        "selpk": selpk, "selq": selq,
        "gpk": np.ascontiguousarray(inputs["L0_ffn_gate_W_packed"]),
        "gsc": np.ascontiguousarray(inputs["L0_ffn_gate_scales"], dtype=np.float32),
        "upk": np.ascontiguousarray(inputs["L0_ffn_up_W_packed"]),
        "usc": np.ascontiguousarray(inputs["L0_ffn_up_scales"], dtype=np.float32),
        "dpk": np.ascontiguousarray(inputs["L0_ffn_down_W_packed"]),
        "dsc": np.ascontiguousarray(inputs["L0_ffn_down_scales"], dtype=np.float32),
    }
    in_maps = []
    for core in range(N_CORES):
        m = dict(common)
        m["x"] = np.ascontiguousarray(att_x[B_PER * core:B_PER * (core + 1)].reshape(T, D), dtype=np.float32)
        in_maps.append(m)
    return in_maps


def kernel(**inputs):
    in_maps = _host_layout(inputs)
    nc = build_program()
    res = run_bass_kernel_spmd(nc, in_maps, core_ids=list(range(N_CORES)))
    out = np.concatenate([r["out"].reshape(B_PER, L, D) for r in res.results], axis=0)
    return out.astype(np.float32)
